# version 09
# baseline (speedup 1.0000x reference)
import math
from contextlib import ExitStack

import numpy as np
import concourse.bass as bass
import concourse.mybir as mybir
from concourse.bass_utils import run_bass_kernel_spmd

F32 = mybir.dt.float32
BF16 = mybir.dt.bfloat16
F16 = mybir.dt.float16
AF = mybir.ActivationFunctionType
ALU = mybir.AluOpType
AX = mybir.AxisListType.X

D = 2048
KC = 16
NT = 1024
NE = 8
RG = [[0, 1, 2, 3], [4, 5, 6, 7]]
ENGINES = ("pe", "act", "dve", "sp", "gp")
ISQ_DB = 1.0 / math.sqrt(128.0)


class PB:
    def __init__(self, nc, sems):
        self.nc = nc
        self.sems = sems
        self.q = {k: [] for k in ENGINES}
        self.cnt = {k: 0 for k in sems}
        self.waited = {e: {} for e in ENGINES}

    def ev(self, sem):
        return (sem, self.cnt[sem])

    DMA_SEMS = ("sx", "sx2", "sxc0", "sxc1", "sg0", "sg1", "sg2", "sy0", "sy1",
                "swr", "syo", "gw", "gb", "gg0", "gg1", "gd0", "gd1",
                "gy0", "gy1")

    def bar(self):
        return [self.ev(s) for s in ("pe", "act", "dve") + self.DMA_SEMS]

    def on(self, eng, waits, fn, inc=None, amt=1):
        wl = []
        for w in (waits or ()):
            if w is None:
                continue
            s, v = w
            if v <= 0 or self.waited[eng].get(s, 0) >= v:
                continue
            self.waited[eng][s] = v
            wl.append((s, v))
        evt = None
        if inc is not None:
            self.cnt[inc] += (amt or 1)
            evt = (inc, self.cnt[inc])
        sems = self.sems

        def closure(e, wl=tuple(wl), fn=fn, inc=inc, amt=amt):
            for s, v in wl:
                e.wait_ge(sems[s], v)
            last = fn(e)
            if inc is not None:
                if amt is None:
                    last.then_inc(sems[inc])
                else:
                    last.then_inc(sems[inc], amt)

        self.q[eng].append(closure)
        return evt


def build_program(stages=4, dbg=False):
    nc = bass.Bass("TRN2", target_bir_lowering=False, debug=False)

    def din(name, shape):
        return nc.dram_tensor(name, shape, F32, kind="ExternalInput").ap()

    x0 = din("x0", [D, NT])
    w_attn = {
        "a": {k: din("wa" + k, [2, 128, KC, 256]) for k in "qkv"}
        | {"o": din("wao", [2, 128, 2, 16, 128])},
        "b": {k: din("wb" + k, [2, 128, KC, 256]) for k in "qkv"}
        | {"o": din("wbo", [2, 128, 2, 16, 128])},
    }
    bias_a = din("bias_a", [8, 5, 128, 512])
    cab_in = din("cab", [1, 16])
    rbx_in = din("rbx", [8, 32])
    maskb_in = din("maskb", [4, 128, 512])
    ident_in = din("ident", [128, 128])
    gw_in = din("gwT", [KC, 128, 9])
    wgu_sh = din("wgu_sh", [44, 128, 2, KC, 128])
    w13s = din("w13s", [NE, 11, 128, 2, KC, 128])
    wdn_sh = din("wdn_sh", [4, 16, 128, 11, 128])
    w2s = din("w2s", [NE, 16, 128, 11, 128])

    y_out = nc.dram_tensor("y", [D, NT], F32, kind="ExternalOutput").ap()
    dbg_t = {}
    if dbg:
        for nm in ("dq", "dk", "dq1", "dk1", "do0", "do1"):
            dbg_t[nm] = nc.dram_tensor(nm, [128, 4096], F16, kind="ExternalOutput").ap()
        dbg_t["dv"] = nc.dram_tensor("dv", [128, 4096], BF16, kind="ExternalOutput").ap()
        dbg_t["dp"] = nc.dram_tensor("dp", [128, 1024], BF16, kind="ExternalOutput").ap()
        dbg_t["db"] = nc.dram_tensor("db", [128, 16], F32, kind="ExternalOutput").ap()
        dbg_t["dbia"] = nc.dram_tensor("dbia", [128, 2560], BF16, kind="ExternalOutput").ap()

    ag_in = [nc.dram_tensor(f"ag_in{k}", [D // 4, NT], F16).ap() for k in range(4)]
    ag_out = [nc.dram_tensor(f"ag_out{k}", [D, NT], F16).ap() for k in range(4)]
    yp = [nc.dram_tensor(f"yp{k}", [D, NT], F32).ap() for k in range(4)]
    yo = [nc.dram_tensor(f"yo{k}", [D // 4, NT], F32).ap() for k in range(4)]
    agv_k = [ag_out[k].rearrange("(r c p) t -> r c p t", c=4, p=128) for k in range(4)]
    ypv_k = [yp[k].rearrange("(r c p) t -> r c p t", c=4, p=128) for k in range(4)]

    with ExitStack() as ctx:
        sem_names = ["pe", "act", "dve", "gpc", "sx", "sx2", "sxc0", "sxc1", "sg0", "sg1", "sg2",
                     "sy0", "sy1", "swr", "syo", "gw", "gb", "gg0", "gg1",
                     "gd0", "gd1", "gy0", "gy1"] + [
                        f"ca{li}{k}" for li in range(2) for k in range(4)] + [
                        f"cr{li}{k}" for li in range(2) for k in range(4)]
        sems = {n: ctx.enter_context(nc.semaphore(n)) for n in sem_names}

        def sball(stk, name, part, cols, dt):
            return stk.enter_context(nc.sbuf_tensor(name, [part, cols], dt))

        X = sball(ctx, "X", 128, KC * 1024, F32)
        SCR = sball(ctx, "SCR", 128, 2560, F32)
        TMP = [sball(ctx, f"TMP{i}", 128, 1024, F32) for i in range(2)]
        AB = sball(ctx, "AB", 128, 2048, F16)
        RB = sball(ctx, "RB", 128, 1024, F32)
        SML = sball(ctx, "SML", 128, 192, F32)
        SMB = sball(ctx, "SMB", 128, 32, BF16)
        SMH = sball(ctx, "SMH", 128, 2, F16)
        IDT = sball(ctx, "IDT", 128, 128, F32)
        B128 = sball(ctx, "B128", 128, 16, F32)
        RBXs = sball(ctx, "RBX", 8, 32, F32)
        RBM = sball(ctx, "RBM", 1, 16, F32)

        PS = [ctx.enter_context(nc.psum_tensor(f"PS{i}", [128, 1024], F32))
              for i in range(4)]
        PA, PBm, PC, PD = PS

        p = PB(nc, sems)
        last_x_ev = [[] for _ in range(KC)]

        ONESB = SMB[:, 0:1]
        ONESH = SMH[:, 0:1]
        ONESF = SML[0:1, 0:128]
        EPS = SML[0:1, 128:129]
        MH = SML[0:1, 129:133]
        BV = SML[0:1, 136:140]
        R_NM = SML[:, 144:145]
        R_SUM = SML[:, 145:146]
        R_RS = SML[:, 146:147]
        R_M1 = SML[:, 147:148]
        R_M2 = SML[:, 148:149]
        RBRED = SML[0:8, 160:161]
        CABS = SML[0:1, 176:192]
        MROW = SCR[0:1, 0:1024]
        TROW = SCR[0:1, 1024:2048]
        NRUN = SCR[0:1, 2048:2560]

        for c in range(KC):
            p.on("sp", [], lambda e, c=c: e.dma_start(
                out=X[:, c * 1024:(c + 1) * 1024],
                in_=x0.rearrange("(c p) t -> p c t", p=128)[:, c, :]),
                inc="sx", amt=16)
        p.on("sp", [], lambda e: e.dma_start(out=IDT[:, :], in_=ident_in),
             inc="sx", amt=16)
        p.on("sp", [], lambda e: e.dma_start(out=RBXs[:, :], in_=rbx_in),
             inc="sx", amt=16)
        p.on("sp", [], lambda e: e.dma_start(out=CABS, in_=cab_in),
             inc="sx", amt=16)
        ev_ld0 = p.ev("sx")
        for c in range(KC):
            last_x_ev[c].append(ev_ld0)
        ev_consts = [
            p.on("dve", [], lambda e: e.memset(SMB[:, 0:1], 1.0), inc="dve"),
            p.on("dve", [], lambda e: e.memset(SMH[:, 0:1], 1.0), inc="dve"),
            p.on("dve", [], lambda e: e.memset(SML[0:1, 0:128], 1.0), inc="dve"),
            p.on("dve", [], lambda e: e.memset(SML[0:1, 128:129], 1e-5), inc="dve"),
        ]
        ev_rbred = p.on("dve", [ev_ld0], lambda e:
                        e.reduce_max(RBRED, RBXs[:, :], axis=AX), inc="dve")
        for h in range(8):
            p.on("sp", [ev_rbred], lambda e, h=h: e.dma_start(
                out=RBM[0:1, h:h + 1], in_=SML[h:h + 1, 160:161]),
                inc="sx2", amt=16)
        ev_rbm = p.ev("sx2")

        def ph_ln(stk, tag, dst, dst_guard=None, dst_dma=None):
            XBh = [sball(stk, f"XB{tag}{i}", 128, 512, BF16) for i in range(2)]
            SQh = [sball(stk, f"SQ{tag}{i}", 128, 512, BF16) for i in range(2)]
            entry = p.bar()
            ev_mm_prev = [None, None]
            for c in range(KC):
                deps = last_x_ev[c] + ev_consts + entry
                for h in range(2):
                    sl = h
                    xch = X[:, c * 1024 + h * 512: c * 1024 + h * 512 + 512]
                    e1 = p.on("dve", deps + [ev_mm_prev[sl]],
                              lambda e, sl=sl, xch=xch:
                              e.tensor_copy(XBh[sl][:, :], xch), inc="dve")
                    e2 = p.on("act", deps + [ev_mm_prev[sl]],
                              lambda e, sl=sl, xch=xch:
                              e.activation(SQh[sl][:, :], xch, AF.Square), inc="act")
                    st_, sp_ = c == 0, c == KC - 1
                    e3 = p.on("pe", [e1, e2], lambda e, sl=sl, h=h, st_=st_, sp_=sp_: (
                        e.matmul(PD[0:1, h * 512:h * 512 + 512], ONESB,
                                 XBh[sl][:, :], start=st_, stop=sp_),
                        e.matmul(PC[0:1, h * 512:h * 512 + 512], ONESB,
                                 SQh[sl][:, :], start=st_, stop=sp_))[-1],
                        inc="pe")
                    ev_mm_prev[sl] = e3
            ev_stats = p.ev("pe")
            e1 = p.on("dve", [ev_stats], lambda e:
                      e.tensor_scalar_mul(MROW, PD[0:1, 0:1024], 1.0 / D), inc="dve")
            e2 = p.on("dve", [e1], lambda e:
                      e.tensor_tensor(TROW, MROW, MROW, ALU.mult), inc="dve")
            e3 = p.on("dve", [e2], lambda e:
                      e.scalar_tensor_tensor(TROW, PC[0:1, 0:1024], 1.0 / D, TROW,
                                             ALU.mult, ALU.subtract), inc="dve")
            e4 = p.on("act", [e3], lambda e:
                      e.activation(TROW, TROW, AF.Ln, bias=EPS), inc="act")
            e5 = p.on("act", [e4], lambda e:
                      e.activation(TROW, TROW, AF.Exp, scale=-0.5), inc="act")
            e6 = p.on("dve", [e5], lambda e:
                      e.scalar_tensor_tensor(MROW, MROW, -1.0, TROW,
                                             ALU.mult, ALU.mult), inc="dve")
            e7 = p.on("pe", [e5, e6] + entry, lambda e: (
                e.matmul(PA[:, 0:512], ONESF, TROW[0:1, 0:512], start=True, stop=True),
                e.matmul(PA[:, 512:1024], ONESF, TROW[0:1, 512:1024], start=True, stop=True),
                e.matmul(PBm[:, 0:512], ONESF, MROW[0:1, 0:512], start=True, stop=True),
                e.matmul(PBm[:, 512:1024], ONESF, MROW[0:1, 512:1024], start=True, stop=True))[-1],
                inc="pe")
            e8 = p.on("act", [e7], lambda e:
                      e.activation(AB[:, 0:1024], PA[:, :], AF.Copy), inc="act")
            e9 = p.on("act", [e8], lambda e:
                      e.activation(AB[:, 1024:2048], PBm[:, :], AF.Copy), inc="act")
            ev_chunks = []
            ev_tmp_prev = [None, None]
            for c in range(KC):
                xc = X[:, c * 1024:(c + 1) * 1024]
                sl = c % 2
                g = list(dst_guard(c)) if dst_guard else []
                t1 = p.on("gp", [e8, e9, ev_tmp_prev[sl]] + last_x_ev[c] + entry,
                          lambda e, xc=xc, sl=sl:
                          e.tensor_tensor(TMP[sl][:, :], xc, AB[:, 0:1024], ALU.mult),
                          inc="gpc", amt=1)
                dstap = dst(c)
                t2 = p.on("dve", [t1] + g, lambda e, sl=sl, dstap=dstap:
                          e.tensor_tensor(dstap, TMP[sl][:, :], AB[:, 1024:2048], ALU.add),
                          inc="dve")
                ev_tmp_prev[sl] = t2
                ev_chunks.append(dst_dma(c, t2) if dst_dma else t2)
            return ev_chunks, [e1, e3, e8, e9]

        def ph_attn(stk, var, li):
            a = var == "a"
            HD = 64 if a else 128
            NHB = 4 if a else 2
            VS = 65 if a else 128
            agsem, rssem = ("cc0", "cc1") if li == 0 else ("cc2", "cc3")
            W = w_attn[var]

            QT = [sball(stk, f"QT{var}{i}", 128, 4096, F16) for i in range(2)]
            KT = [sball(stk, f"KT{var}{i}", 128, 4096, F16) for i in range(2)]
            V4 = sball(stk, f"V4{var}", 128, 32 * VS * NHB, BF16)
            OUT = [sball(stk, f"OUT{var}{i}", 128, 4096, F16) for i in range(2)]
            PT = [sball(stk, f"PT{var}{i}", 128, 1024, BF16) for i in range(3)]
            WAB = sball(stk, f"WAB{var}", 128, 8192, F16)
            BIA = sball(stk, f"BIA{var}", 128, 2560 if a else 2048, BF16)
            XGT = [sball(stk, f"XGT{var}{i}", 128, 1024, F16) for i in range(3)]
            XNC = [sball(stk, f"XNC{var}{i}", 128, 1024, F16) for i in range(2)]
            MIX = sball(stk, f"MIX{var}", 128, 512, F32) if a else None
            YST = [sball(stk, f"YST{var}{i}", 128, 512, F32) for i in range(2)]
            SQB = [sball(stk, f"SQB{var}{i}", 128, 512, F16) for i in range(2)]
            V4r = V4.ap().rearrange("p (h r) -> p h r", h=NHB)

            ev_xnc_dma = [None]
            ev_ag = [None] * 4
            ag_guard = [p.ev(s) for s in ("sxc0", "sxc1", "sg0", "sg1", "sg2")]

            ev_xnc_dma2 = [None, None]

            def xnc_guard(c):
                return [ev_xnc_dma2[c % 2]]

            def xnc_dma(c, ev):
                w = [ev]
                if li == 1:
                    w.append((f"ca0{c // 4}", 1))
                out = ag_in[c // 4].rearrange("(c p) t -> p c t", p=128)[:, c % 4, :]
                evd = p.on("sp", w, lambda e, out=out, c=c:
                           e.dma_start(out=out, in_=XNC[c % 2][:, :]), inc=f"sxc{c % 2}", amt=16)
                ev_xnc_dma2[c % 2] = evd
                if c % 4 == 3:
                    k = c // 4
                    p.on("gp", [ev_xnc_dma2[0], ev_xnc_dma2[1]] + ag_guard,
                         lambda e, k=k: e.collective_compute(
                        "AllGather", ALU.bypass, replica_groups=RG,
                        ins=[ag_in[k]], outs=[ag_out[k]]), inc=f"ca{li}{k}", amt=None)
                    ev_ag[k] = (f"ca{li}{k}", 1)
                return evd

            phase_bar = p.bar()
            ln_evs, ln_guard = ph_ln(stk, f"A{var}", lambda c: XNC[c % 2][:, :],
                                     xnc_guard, xnc_dma)

            ev_masks = None
            if not a:
                ev_masks = p.on("gp", p.bar(), lambda e: e.dma_start(
                    out=BIA.ap().rearrange("p (d t) -> p d t", d=4),
                    in_=maskb_in.rearrange("d p t -> p d t")),
                    inc="gb", amt=16)

            ev_y_hb_done = None
            ev_nm_prev = None
            ev_yst = [None, None]
            ev_bc_read = None
            ev_hb_prev_dve = []
            for hb in range(2):
                entry_pe = p.ev("pe")
                wdeps = [entry_pe]
                p.on("gp", wdeps, lambda e, hb=hb:
                     e.dma_start(out=WAB[:, 0:4096], in_=W["q"][hb]),
                     inc="gw", amt=16)
                evw = p.on("gp", [], lambda e, hb=hb:
                           e.dma_start(out=WAB[:, 4096:8192], in_=W["k"][hb]),
                           inc="gw", amt=16)
                qk_mm = {}
                ev_evac_last = None
                first = True
                for r in range(4):
                    for kc in range(KC):
                        i = r * KC + kc
                        sl = i % 3
                        srcx = agv_k[kc // 4][r, kc % 4, :, :]
                        evx = p.on("sp", [ev_ag[kc // 4], qk_mm.get(i - 3), entry_pe],
                                   lambda e, srcx=srcx, sl=sl:
                                   e.dma_start(out=XGT[sl][:, :], in_=srcx),
                                   inc=f"sg{sl}", amt=16)
                        st_, sp_ = kc == 0, kc == KC - 1
                        deps = [evx, evw]
                        if kc == 0:
                            deps.append(ev_evac_last)
                        if first:
                            deps += ln_guard + list(ev_hb_prev_dve)
                            first = False

                        def qkmm(e, sl=sl, kc=kc, st_=st_, sp_=sp_):
                            last = None
                            for half in range(2):
                                rhs = XGT[sl][:, half * 512:half * 512 + 512]
                                qp = [PA, PBm][half]
                                kp = [PC, PD][half]
                                for pi in range(2):
                                    last = e.matmul(qp[:, pi * 512:pi * 512 + 512],
                                                    WAB[:, kc * 256 + pi * 128:kc * 256 + pi * 128 + 128],
                                                    rhs, start=st_, stop=sp_)
                                    last = e.matmul(kp[:, pi * 512:pi * 512 + 512],
                                                    WAB[:, 4096 + kc * 256 + pi * 128:4096 + kc * 256 + pi * 128 + 128],
                                                    rhs, start=st_, stop=sp_)
                            return last
                        qk_mm[i] = p.on("pe", deps, qkmm, inc="pe")
                    eW = qk_mm[r * KC + KC - 1]
                    for half in range(2):
                        tc = 2 * r + half
                        qp = [PA, PBm][half]
                        kp = [PC, PD][half]
                        for pi in range(2):
                            p.on("dve", [eW], lambda e, pi=pi, tc=tc, qp=qp:
                                 e.tensor_copy(QT[pi][:, tc * 512:tc * 512 + 512],
                                               qp[:, pi * 512:pi * 512 + 512]), inc="dve")
                            ev_evac_last = p.on("dve", [eW], lambda e, pi=pi, tc=tc, kp=kp:
                                                e.tensor_copy(KT[pi][:, tc * 512:tc * 512 + 512],
                                                              kp[:, pi * 512:pi * 512 + 512]),
                                                inc="dve")
                ev_qk_done_pe = p.ev("pe")
                ev_qk_evacs = p.ev("dve")

                evwv = p.on("gp", [ev_qk_done_pe], lambda e, hb=hb:
                            e.dma_start(out=WAB[:, 0:4096], in_=W["v"][hb]),
                            inc="gw", amt=16)
                ev_vset = None
                if a:
                    ev_vset = p.on("dve", [p.ev("pe")], lambda e:
                                   e.memset(V4[:, :], 1.0), inc="dve")
                PSQ0 = [PC[:, 0:256], PC[:, 512:768], PD[:, 0:256], PD[:, 512:768]]
                PSQ1 = [PA[:, 0:256], PA[:, 512:768], PBm[:, 0:256], PBm[:, 512:768]]
                v_mm = {}
                ev_v_evac_last = None
                for r in range(4):
                    for kc in range(KC):
                        i = r * KC + kc
                        sl = i % 3
                        srcx = agv_k[kc // 4][r, kc % 4, :, :]
                        evx = p.on("sp", [v_mm.get(i - 3), ev_qk_done_pe, ev_qk_evacs],
                                   lambda e, srcx=srcx, sl=sl:
                                   e.dma_start(out=XGT[sl][:, :], in_=srcx),
                                   inc=f"sg{sl}", amt=16)
                        st_, sp_ = kc == 0, kc == KC - 1
                        deps = [evx, evwv, ev_vset]
                        if kc == 0:
                            deps.append(ev_v_evac_last)

                        def vmm2(e, sl=sl, kc=kc, st_=st_, sp_=sp_):
                            last = None
                            for half in range(2):
                                psq = [PSQ0, PSQ1][half]
                                for sub in range(4):
                                    last = e.matmul(psq[sub],
                                                    XGT[sl][:, half * 512 + sub * 128:half * 512 + sub * 128 + 128],
                                                    WAB[:, kc * 256:kc * 256 + 256],
                                                    start=st_, stop=sp_)
                            return last
                        v_mm[i] = p.on("pe", deps, vmm2, inc="pe")
                    eW = v_mm[r * KC + KC - 1]
                    for half in range(2):
                        tc = 2 * r + half
                        psq = [PSQ0, PSQ1][half]
                        for sub in range(4):
                            kt = tc * 4 + sub
                            dstv = V4r[:, :, kt * VS: kt * VS + HD]
                            srcv = psq[sub].rearrange("p (h d) -> p h d", h=NHB)
                            ev_v_evac_last = p.on("dve", [eW, ev_vset],
                                                  lambda e, dstv=dstv, srcv=srcv:
                                                  e.tensor_copy(dstv, srcv), inc="dve")
                ev_vdone_pe = p.ev("pe")
                ev_vdone = p.ev("dve")

                ev_norm_read = None
                s2_guard = {i: [] for i in range(6)}
                ev_bias_ready_h = {}
                ev_biat_h = {}
                ev_sq_mm_prev = [None, None]
                ev_run_prev = ev_nm_prev
                e3chain = None
                ev_e4_prev = None
                for hh in range(NHB):
                    pi, hoff = (hh // 2, (hh % 2) * 64) if a else (hh, 0)
                    hglob = hb * NHB + hh
                    ev_nm = []
                    for ti, Tsrc in enumerate((QT, KT)):
                        for cc in range(8):
                            sl = cc % 2
                            srcap = Tsrc[pi][hoff:hoff + HD, cc * 512:cc * 512 + 512]
                            esq = p.on("act", [ev_qk_evacs, ev_sq_mm_prev[sl]],
                                       lambda e, srcap=srcap, sl=sl:
                                       e.activation(SQB[sl][0:HD, :], srcap, AF.Square),
                                       inc="act")
                            emm = p.on("pe", [esq, ev_vdone, ev_run_prev],
                                       lambda e, sl=sl:
                                       e.matmul(PC[0:1, 0:512], ONESH[0:HD, 0:1],
                                                SQB[sl][0:HD, :], start=True, stop=True),
                                       inc="pe")
                            ev_sq_mm_prev[sl] = emm
                            if cc == 0:
                                ev_run_prev = p.on("dve", [emm, ev_run_prev], lambda e:
                                                   e.tensor_copy(NRUN, PC[0:1, 0:512]),
                                                   inc="dve")
                            else:
                                ev_run_prev = p.on("dve", [emm, ev_run_prev], lambda e:
                                                   e.tensor_tensor(NRUN, NRUN,
                                                                   PC[0:1, 0:512], ALU.max),
                                                   inc="dve")
                        ev_nm.append(p.on("dve", [ev_run_prev], lambda e, ti=ti:
                                          e.reduce_max(MH[0:1, ti:ti + 1], NRUN, axis=AX),
                                          inc="dve"))
                        ev_run_prev = ev_nm[-1]
                    e0 = p.on("dve", ev_nm + [e3chain], lambda e:
                              e.tensor_tensor(MH[0:1, 2:3], MH[0:1, 0:1], MH[0:1, 1:2],
                                              ALU.mult), inc="dve")
                    e1 = p.on("act", [e0], lambda e:
                              e.activation(MH[0:1, 2:3], MH[0:1, 2:3], AF.Ln), inc="act")
                    sm = 1.0 if a else ISQ_DB
                    e1b = p.on("act", [e1], lambda e:
                               e.activation(MH[0:1, 2:3], MH[0:1, 2:3], AF.Exp, scale=0.5),
                               inc="act")
                    e2 = p.on("dve", [e1b], lambda e, sm=sm:
                              e.tensor_scalar_mul(MH[0:1, 2:3], MH[0:1, 2:3], sm),
                              inc="dve")
                    if a:
                        e2 = p.on("dve", [e2, ev_rbm], lambda e, hglob=hglob:
                                  e.tensor_tensor(MH[0:1, 2:3], MH[0:1, 2:3],
                                                  RBM[0:1, hglob:hglob + 1], ALU.add),
                                  inc="dve")
                    e2b = p.on("dve", [e2], lambda e:
                               e.tensor_scalar_sub(MH[0:1, 2:3], MH[0:1, 2:3], 60.0),
                               inc="dve")
                    e3 = p.on("dve", [e2b, ev_e4_prev], lambda e:
                              e.tensor_scalar_mul(BV[0:1, 0:1], MH[0:1, 2:3], -1.0),
                              inc="dve")
                    if a:
                        e3 = p.on("dve", [e3, ev_ld0], lambda e, hglob=hglob: (
                            e.scalar_tensor_tensor(BV[0:1, 1:2], MH[0:1, 2:3], -1.0,
                                                   CABS[0:1, hglob:hglob + 1],
                                                   ALU.mult, ALU.add),
                            e.scalar_tensor_tensor(BV[0:1, 2:3], MH[0:1, 2:3], -1.0,
                                                   CABS[0:1, 8 + hglob:9 + hglob],
                                                   ALU.mult, ALU.add))[-1], inc="dve")
                    e3chain = e3
                    nbv = 3 if a else 1
                    e4 = p.on("pe", [e3, ev_vdone, ev_bc_read,
                                     ev_bias_ready_h.get(hh - 1)], lambda e, nbv=nbv:
                              e.matmul(PD[:, 768:768 + nbv], ONESF, BV[0:1, 0:nbv],
                                       start=True, stop=True), inc="pe")
                    ev_bias_ready_h[hh] = p.on("act", [e4], lambda e, nbv=nbv, hh=hh:
                                               e.activation(B128[:, nbv * hh:nbv * hh + nbv],
                                                            PD[:, 768:768 + nbv],
                                                            AF.Copy), inc="act")
                    ev_e4_prev = e4
                ev_nm_prev = ev_run_prev
                ev_norm_read = ev_nm_prev
                s2_guard[4] = s2_guard[5] = [ev_bias_ready_h[NHB - 1]]


                for hh in range(NHB):
                    pi, hoff = (hh // 2, (hh % 2) * 64) if a else (hh, 0)
                    hglob = hb * NHB + hh
                    ev_bias_ready = ev_bias_ready_h[hh]
                    ev_biat = None
                    if a:
                        ev_biat = p.on("gp", [p.ev("act"), p.ev("dve")],
                                       lambda e, hglob=hglob: e.dma_start(
                                           out=BIA.ap().rearrange("p (d t) -> p d t", d=5),
                                           in_=bias_a[hglob].rearrange("d p t -> p d t")),
                                       inc="gb", amt=16)
                    bvbase = (3 if a else 1) * hh
                    for qc in range(4):
                        nj = (32 if a else 8 * qc + 8) * 2
                        ktmax = nj // 2

                        def offh(j):
                            return (j // 2) * 128 - (qc * 1024 + (j % 2) * 512)
                        if a:
                            live = list(range(nj))
                        else:
                            live = [j for j in range(nj) if offh(j) < 512]
                        fl = {0: None, 1: None}
                        ll = {0: None, 1: None}
                        for j in live:
                            if fl[j % 2] is None:
                                fl[j % 2] = j
                            ll[j % 2] = j
                        SLOT = [PA[:, 0:512], PA[:, 512:1024],
                                PBm[:, 0:512], PBm[:, 512:1024]]
                        if a:
                            SLOT = SLOT + [PD[:, 0:512], PD[:, 512:1024]]
                        NSLOT = len(SLOT)
                        ev_s = {}
                        ev_exp = {}
                        ev_av = {}
                        exp_read = {}

                        def emit_s(j, si):
                            kt, half = j // 2, j % 2
                            sb_ = SLOT[si % NSLOT]
                            lhs = KT[pi][hoff:hoff + HD, kt * 128:kt * 128 + 128]
                            rhs = QT[pi][hoff:hoff + HD,
                                         qc * 1024 + half * 512: qc * 1024 + half * 512 + 512]
                            deps = [ev_bias_ready, exp_read.get(si - NSLOT)]
                            if si < NSLOT:
                                deps += s2_guard[si % NSLOT]
                            ev_s[j] = p.on("pe", deps, lambda e, lhs=lhs, rhs=rhs, sb_=sb_:
                                           e.matmul(sb_, lhs, rhs, start=True, stop=True),
                                           inc="pe")

                        def emit_exp(j, si):
                            kt, half = j // 2, j % 2
                            sb_ = SLOT[si % NSLOT]
                            pt = PT[(si % 6) // 2]
                            ptap = pt[:, (si % 2) * 512:(si % 2) * 512 + 512]
                            dep_pt = [ev_av.get(si - 6)]
                            if a:
                                off = offh(j)
                                if off in (-128, 0, 128, 256, 384):
                                    dd = (off + 128) // 128
                                    em = p.on("dve", [ev_s[j], ev_biat, exp_read.get("mix")],
                                              lambda e, sb_=sb_, dd=dd:
                                              e.tensor_tensor(MIX[:, :], sb_,
                                                              BIA[:, dd * 512:dd * 512 + 512],
                                                              ALU.add), inc="dve")
                                    exp_read[si] = em
                                    eh = p.on("act", [em] + dep_pt,
                                              lambda e, ptap=ptap, bvbase=bvbase:
                                              e.activation(ptap, MIX[:, :], AF.Exp,
                                                           bias=B128[:, bvbase:bvbase + 1]),
                                              inc="act")
                                    exp_read["mix"] = eh
                                else:
                                    bcol = 1 if off >= 512 else 2
                                    eh = p.on("act", [ev_s[j]] + dep_pt,
                                              lambda e, ptap=ptap, sb_=sb_, bcol=bcol, bvbase=bvbase:
                                              e.activation(ptap, sb_, AF.Exp,
                                                           bias=B128[:, bvbase + bcol:bvbase + bcol + 1]),
                                              inc="act")
                                    exp_read[si] = eh
                                ev_exp[j] = eh
                            else:
                                ee = p.on("act", [ev_s[j]] + dep_pt,
                                          lambda e, ptap=ptap, sb_=sb_, bvbase=bvbase:
                                          e.activation(ptap, sb_, AF.Exp,
                                                       bias=B128[:, bvbase:bvbase + 1],
                                                       scale=ISQ_DB),
                                          inc="act")
                                exp_read[si] = ee
                                off = offh(j)
                                if 0 <= off < 512:
                                    dd = off // 128
                                    ee = p.on("dve", [ee, ev_masks],
                                              lambda e, ptap=ptap, dd=dd:
                                              e.tensor_tensor(ptap, ptap,
                                                              BIA[:, dd * 512:dd * 512 + 512],
                                                              ALU.mult), inc="dve")
                                ev_exp[j] = ee

                        def emit_av(j, si):
                            kt, half = j // 2, j % 2
                            pt = PT[(si % 6) // 2]
                            ptap = pt[:, (si % 2) * 512:(si % 2) * 512 + 512]
                            vl = V4r[:, hh, kt * VS: kt * VS + (VS if a else 128)]
                            st_, sp_ = j == fl[half], j == ll[half]
                            avdeps = [ev_exp[j]]
                            if si < 2:
                                avdeps += [ev_norm_read, ev_vdone]
                                if not a:
                                    avdeps.append(ev_bc_read)

                            def avfn(e, vl=vl, ptap=ptap, st_=st_, sp_=sp_, half=half):
                                last = e.matmul(PC[0:(65 if a else 128),
                                                   half * 512:half * 512 + 512],
                                                vl, ptap, start=st_, stop=sp_)
                                if not a:
                                    last = e.matmul(PD[0:1, half * 512:half * 512 + 512],
                                                    ONESB, ptap, start=st_, stop=sp_)
                                return last
                            ev_av[si] = p.on("pe", avdeps, avfn, inc="pe")
                        nl = len(live)
                        LE = 4
                        for si in range(min(NSLOT, nl)):
                            emit_s(live[si], si)
                            if si < LE:
                                emit_exp(live[si], si)
                        for si in range(nl + 2):
                            if si >= 2:
                                emit_av(live[si - 2], si - 2)
                            if si + NSLOT < nl:
                                emit_s(live[si + NSLOT], si + NSLOT)
                            if LE <= si + LE < nl:
                                emit_exp(live[si + LE], si + LE)
                        lsrc = PC[64:65, 0:1024] if a else PD[0:1, 0:1024]
                        er = p.on("dve", [ev_av[nl - 1], ev_av[nl - 2]],
                                  lambda e, lsrc=lsrc:
                                  e.reciprocal(RB[0:1, 0:1024], lsrc), inc="dve")
                        ebdeps = [er, ev_bc_read] + [exp_read.get(nl - k)
                                                      for k in range(1, NSLOT + 1)]
                        eb = p.on("pe", ebdeps, lambda e: (
                            e.matmul(PD[0:(64 if a else 128), 0:512], ONESF[0:1, 0:(64 if a else 128)],
                                     RB[0:1, 0:512], start=True, stop=True),
                            e.matmul(PD[0:(64 if a else 128), 512:1024], ONESF[0:1, 0:(64 if a else 128)],
                                     RB[0:1, 512:1024], start=True, stop=True))[-1],
                            inc="pe")
                        ec = p.on("act", [eb], lambda e:
                                  e.activation(RB[:, :], PD[:, :], AF.Copy), inc="act")
                        ev_bc_read = ec
                        dsto = OUT[pi][hoff:hoff + (64 if a else 128),
                                       qc * 1024:qc * 1024 + 1024]
                        eo = p.on("dve", [ec], lambda e, dsto=dsto:
                                  e.tensor_tensor(dsto, PC[0:(64 if a else 128), 0:1024],
                                                  RB[0:(64 if a else 128), 0:1024],
                                                  ALU.mult), inc="dve")
                        ev_norm_read = eo
                        for back in range(1, NSLOT + 1):
                            s2_guard[(nl - back) % NSLOT] = [exp_read.get(nl - back)]
                        if a:
                            s2_guard[4] = s2_guard[4] + [ec]
                            s2_guard[5] = s2_guard[5] + [ec]

                evwo = p.on("gp", [p.ev("pe")], lambda e, hb=hb:
                            e.dma_start(out=WAB[:, 0:4096], in_=W["o"][hb]),
                            inc="gw", amt=16)
                ev_all_dve = p.ev("dve")
                ev_all_act = p.ev("act")
                ev_ee = {}
                if hb == 1:
                    ev_y_hb0 = ev_y_hb_done
                for oc in range(16):
                    for tc in range(8):
                        i = oc * 8 + tc
                        sl = i % 2
                        yps = [PA, PBm][sl]
                        deps = [evwo, ev_all_dve, ev_all_act, ev_ee.get(i - 2)]
                        if i == 0:
                            deps += s2_guard[0] + s2_guard[1] + s2_guard[2] + s2_guard[3]
                        emm = p.on("pe", deps, lambda e, oc=oc, tc=tc, yps=yps: (
                            e.matmul(yps[:, 0:512],
                                     WAB[:, oc * 128:oc * 128 + 128],
                                     OUT[0][:, tc * 512:tc * 512 + 512],
                                     start=True, stop=False),
                            e.matmul(yps[:, 0:512],
                                     WAB[:, 2048 + oc * 128:2048 + oc * 128 + 128],
                                     OUT[1][:, tc * 512:tc * 512 + 512],
                                     start=False, stop=True))[-1],
                            inc="pe")
                        ee = p.on("dve", [emm, ev_yst[sl]], lambda e, sl=sl, yps=yps:
                                  e.tensor_copy(YST[sl][:, :], yps[:, 0:512]), inc="dve")
                        ev_ee[i] = ee
                        dst = ypv_k[oc // 4][tc // 2, oc % 4, :, (tc % 2) * 512:(tc % 2) * 512 + 512]
                        op = ALU.bypass if hb == 0 else ALU.add
                        ydeps = [ee]
                        if hb == 1 and oc == 0 and tc == 0:
                            ydeps += ev_y_hb0
                        if li == 1 and hb == 0 and tc == 0 and oc % 4 == 0:
                            ydeps.append((f"cr0{oc // 4}", 1))
                        evd = p.on("gp", ydeps, lambda e, dst=dst, sl=sl, op=op:
                                   e.dma_start(out=dst, in_=YST[sl][:, :], accum_op=op),
                                   inc=f"gy{sl}", amt=16)
                        ev_yst[sl] = evd
                    if hb == 1 and oc % 4 == 3:
                        k = oc // 4
                        p.on("gp", [p.ev("gy0"), p.ev("gy1")],
                             lambda e, k=k: e.collective_compute(
                                 "ReduceScatter", ALU.add, replica_groups=RG,
                                 ins=[yp[k]], outs=[yo[k]]), inc=f"cr{li}{k}", amt=None)
                ev_y_hb_done = [p.ev("gy0"), p.ev("gy1")]
            ev_prev = [None, None]
            for c in range(KC):
                sl = c % 2
                eld = p.on("sp", [(f"cr{li}{c // 4}", 1), ev_prev[sl]],
                           lambda e, c=c, sl=sl: e.dma_start(
                               out=TMP[sl][:, :],
                               in_=yo[c // 4].rearrange("(c p) t -> p c t", p=128)[:, c % 4, :]),
                           inc=f"sy{sl}", amt=16)
                xc = X[:, c * 1024:(c + 1) * 1024]
                ea = p.on("dve", [eld] + last_x_ev[c], lambda e, xc=xc, sl=sl:
                          e.tensor_tensor(xc, xc, TMP[sl][:, :], ALU.add), inc="dve")
                ev_prev[sl] = ea
                last_x_ev[c] = [ea]

        def ph_ffn(stk, li):
            XN = sball(stk, f"XN{li}", 128, KC * 1024, BF16)
            H = sball(stk, f"H{li}", 128, 11 * 1024, BF16)
            WG = [sball(stk, f"WG{li}{i}", 128, 2 * KC * 128, BF16) for i in range(2)]
            WD = [sball(stk, f"WD{li}{i}", 128, 11 * 128, BF16) for i in range(2)]
            WBC = sball(stk, f"WBC{li}", 128, 1024, F32)
            SGB = sball(stk, f"SGB{li}", 128, 1024, F32)
            GWT = sball(stk, f"GWT{li}", 128, 9 * KC, BF16)
            LGS = sball(stk, f"LGS{li}", 128, 80, F32)
            WSG = sball(stk, f"WSG{li}", 128, 9, F32)
            P8 = sball(stk, f"P8{li}", 128, 24, F32)
            WRO = sball(stk, f"WRO{li}", 9, 1024, F32)
            WR1 = sball(stk, f"WR1{li}", 1, 1024, F32)

            ln_evs, ln_guard = ph_ln(stk, f"F{li}", lambda c: XN[:, c * 1024:(c + 1) * 1024])
            ev_gw = p.on("gp", p.bar(), lambda e: e.dma_start(
                out=GWT.ap().rearrange("p (c n) -> p c n", c=KC),
                in_=gw_in.rearrange("c p n -> p c n")), inc="gw", amt=16)

            ev_lg = {}
            ev_lg_ev = {}
            for tb in range(8):
                poff = (tb % 2) * 512

                def rmm(e, tb=tb, poff=poff):
                    last = None
                    for kc in range(KC):
                        last = e.matmul(PD[:, poff:poff + 9],
                                        XN[:, kc * 1024 + tb * 128: kc * 1024 + tb * 128 + 128],
                                        GWT[:, kc * 9:kc * 9 + 9],
                                        start=(kc == 0), stop=(kc == KC - 1))
                    return last
                emm = p.on("pe", ln_evs + ln_guard + [ev_gw, ev_lg_ev.get(tb - 2)], rmm, inc="pe")
                ev_lg[tb] = emm
                ev_lg_ev[tb] = p.on("act", [emm], lambda e, tb=tb, poff=poff:
                                    e.activation(LGS[:, tb * 9:tb * 9 + 9],
                                                 PD[:, poff:poff + 9], AF.Copy), inc="act")
            ev_tr = []
            ev_tr_prev = None
            for tb in range(8):
                lg = LGS[:, tb * 9:tb * 9 + 9]
                e1 = p.on("dve", [ev_lg_ev[tb], ev_tr_prev], lambda e, lg=lg:
                          e.reduce_max(R_NM, lg[:, 1:9], axis=AX, negate=True), inc="dve")
                e2 = p.on("act", [e1], lambda e, lg=lg:
                          e.activation(P8[:, 0:8], lg[:, 1:9], AF.Exp, bias=R_NM), inc="act")
                e3 = p.on("dve", [e2], lambda e:
                          e.reduce_sum(R_SUM, P8[:, 0:8], axis=AX), inc="dve")
                e4 = p.on("dve", [e3], lambda e:
                          e.reciprocal(R_RS, R_SUM), inc="dve")
                e5 = p.on("dve", [e4], lambda e:
                          e.tensor_scalar_mul(P8[:, 0:8], P8[:, 0:8], R_RS), inc="dve")
                e6 = p.on("dve", [e5], lambda e:
                          e.reduce_max(R_M1, P8[:, 0:8], axis=AX), inc="dve")
                e7 = p.on("dve", [e6], lambda e:
                          e.tensor_single_scalar(P8[:, 8:16], P8[:, 0:8], R_M1, ALU.is_ge),
                          inc="dve")
                e8 = p.on("dve", [e7], lambda e:
                          e.scalar_tensor_tensor(P8[:, 16:24], P8[:, 8:16], -1e30,
                                                 P8[:, 0:8], ALU.mult, ALU.add), inc="dve")
                e9 = p.on("dve", [e8], lambda e:
                          e.reduce_max(R_M2, P8[:, 16:24], axis=AX), inc="dve")
                e10 = p.on("dve", [e9], lambda e:
                           e.tensor_single_scalar(P8[:, 8:16], P8[:, 0:8], R_M2, ALU.is_ge),
                           inc="dve")
                e11 = p.on("dve", [e10], lambda e:
                           e.tensor_tensor(WSG[:, 0:8], P8[:, 0:8], P8[:, 8:16], ALU.mult),
                           inc="dve")
                e12 = p.on("act", [e11], lambda e, lg=lg:
                           e.activation(WSG[:, 8:9], lg[:, 0:1], AF.Sigmoid), inc="act")
                etr = p.on("pe", [e11, e12, ev_ld0], lambda e, tb=tb:
                           e.transpose(PC[0:9, tb * 128:tb * 128 + 128], WSG[:, :],
                                       IDT[:, :]), inc="pe")
                ev_tr_prev = etr
                ev_tr.append(etr)
            ewro = p.on("dve", ev_tr, lambda e:
                        e.tensor_copy(WRO[:, :], PC[0:9, 0:1024]), inc="dve")
            esg0 = p.on("sp", [ewro], lambda e:
                        e.dma_start(out=WR1[0:1, :], in_=WRO[8:9, :]), inc="swr", amt=16)
            esg1 = p.on("pe", [esg0], lambda e: (
                e.matmul(PD[:, 0:512], ONESF, WR1[0:1, 0:512], start=True, stop=True),
                e.matmul(PD[:, 512:1024], ONESF, WR1[0:1, 512:1024], start=True, stop=True))[-1],
                inc="pe")
            esg2 = p.on("act", [esg1], lambda e:
                        e.activation(SGB[:, :], PD[:, :], AF.Copy), inc="act")

            branches = [("sh", j) for j in range(4)] + [("ex", k) for k in range(NE)]
            ev_wr1_mm = esg1
            ev_wbc_rd = None
            wg_used = [None, None]
            wd_used = [None, None]
            silu_rd = [None, None]
            tt1_rd = [None, None]
            tmp_used = [None, None]
            dn_tt = [None, None]
            ev_h_rd_pe = None
            ev_pd_rd = esg2
            for kind, idx in branches:
                if kind == "ex":
                    ex0 = p.on("sp", [ewro, ev_wr1_mm], lambda e, idx=idx:
                               e.dma_start(out=WR1[0:1, :], in_=WRO[idx:idx + 1, :]),
                               inc="swr", amt=16)
                    ex1 = p.on("pe", [ex0, ev_pd_rd, dn_tt[1]], lambda e: (
                        e.matmul(PD[:, 0:512], ONESF, WR1[0:1, 0:512], start=True, stop=True),
                        e.matmul(PD[:, 512:1024], ONESF, WR1[0:1, 512:1024], start=True, stop=True))[-1],
                        inc="pe")
                    ev_wr1_mm = ex1
                    scb = p.on("act", [ex1, ev_wbc_rd], lambda e:
                               e.activation(WBC[:, :], PD[:, :], AF.Copy), inc="act")
                    ev_pd_rd = scb
                    bc, ev_bc = WBC, scb
                else:
                    bc, ev_bc = SGB, esg2
                ev_h_new = []
                for op in range(11):
                    wsl = op % 2
                    src = (wgu_sh[idx * 11 + op] if kind == "sh" else w13s[idx, op])
                    evw = p.on("gp", [wg_used[wsl]], lambda e, src=src, wsl=wsl:
                               e.dma_start(out=WG[wsl].ap().rearrange(
                                   "p (s c m) -> p s c m", s=2, c=KC), in_=src),
                               inc=f"gg{wsl}", amt=16)
                    ev_s = {}
                    for s, dstp in ((0, PA), (1, PBm)):
                        guard = silu_rd[0] if s == 0 else tt1_rd[0]

                        def mmfn(e, wsl=wsl, s=s, dstp=dstp):
                            last = None
                            for kc in range(KC):
                                for half in range(2):
                                    last = e.matmul(
                                        dstp[:, half * 512:half * 512 + 512],
                                        WG[wsl][:, (s * KC + kc) * 128:(s * KC + kc) * 128 + 128],
                                        XN[:, kc * 1024 + half * 512: kc * 1024 + half * 512 + 512],
                                        start=(kc == 0), stop=(kc == KC - 1))
                            return last
                        ev_s[s] = p.on("pe", ln_evs + ln_guard + [evw, guard], mmfn, inc="pe")
                    wg_used[wsl] = ev_s[1]
                    sl = op % 2
                    es1 = p.on("act", [ev_s[0], tmp_used[sl]], lambda e, sl=sl:
                               e.activation(TMP[sl][:, :], PA[:, :], AF.Silu), inc="act")
                    silu_rd[0] = es1
                    es2 = p.on("dve", [es1, ev_s[1]], lambda e, sl=sl:
                               e.tensor_tensor(TMP[sl][:, :], TMP[sl][:, :], PBm[:, :],
                                               ALU.mult), inc="dve")
                    tt1_rd[0] = es2
                    es3 = p.on("dve", [es2, ev_bc, ev_h_rd_pe], lambda e, sl=sl, op=op, bc=bc:
                               e.tensor_tensor(H[:, op * 1024:op * 1024 + 1024],
                                               TMP[sl][:, :], bc[:, :], ALU.mult),
                               inc="dve")
                    tmp_used[sl] = es3
                    ev_h_new.append(es3)
                ev_wbc_rd = ev_h_new[-1]
                for oc in range(16):
                    wsl = oc % 2
                    src = (wdn_sh[idx, oc] if kind == "sh" else w2s[idx, oc])
                    evw = p.on("gp", [wd_used[wsl]], lambda e, src=src, wsl=wsl:
                               e.dma_start(out=WD[wsl].ap().rearrange(
                                   "p (c m) -> p c m", c=11), in_=src),
                               inc=f"gd{wsl}", amt=16)
                    dps = [PC, PD][oc % 2]
                    dndeps = [evw, dn_tt[oc % 2]] + ev_h_new
                    if oc % 2 == 1:
                        dndeps.append(ev_pd_rd)

                    def dnmm(e, wsl=wsl, dps=dps):
                        last = None
                        for jc in range(11):
                            for half in range(2):
                                last = e.matmul(
                                    dps[:, half * 512:half * 512 + 512],
                                    WD[wsl][:, jc * 128:jc * 128 + 128],
                                    H[:, jc * 1024 + half * 512: jc * 1024 + half * 512 + 512],
                                    start=(jc == 0), stop=(jc == 10))
                            pass
                        return last
                    emm = p.on("pe", dndeps, dnmm, inc="pe")
                    wd_used[wsl] = emm
                    xc = X[:, oc * 1024:(oc + 1) * 1024]
                    ea = p.on("dve", [emm] + last_x_ev[oc], lambda e, xc=xc, dps=dps:
                              e.tensor_tensor(xc, xc, dps[:, :], ALU.add), inc="dve")
                    dn_tt[oc % 2] = ea
                    last_x_ev[oc] = [ea]
                ev_h_rd_pe = wd_used[1]

        if stages >= 1:
            with ExitStack() as s1:
                ph_attn(s1, "a", 0)
        if stages >= 2:
            with ExitStack() as s2:
                ph_ffn(s2, 0)
        if stages >= 3:
            with ExitStack() as s3:
                ph_attn(s3, "b", 1)
        if stages >= 4:
            with ExitStack() as s4:
                ph_ffn(s4, 1)

        for c in range(KC):
            p.on("sp", last_x_ev[c], lambda e, c=c: e.dma_start(
                out=y_out.rearrange("(c p) t -> p c t", p=128)[:, c, :],
                in_=X[:, c * 1024:(c + 1) * 1024]), inc="syo", amt=16)
        ev_final = p.ev("syo")
        p.on("sp", [ev_final], lambda e: None)

        with nc.Block() as block:
            block.tensor(lambda e: [f(e) for f in p.q["pe"]])
            block.scalar(lambda e: [f(e) for f in p.q["act"]])
            block.vector(lambda e: [f(e) for f in p.q["dve"]])
            block.sync(lambda e: [f(e) for f in p.q["sp"]])
            block.gpsimd(lambda e: [f(e) for f in p.q["gp"]])

    return nc



def _bucket_rp(delta):
    delta = np.asarray(delta, dtype=np.int64)
    rp = np.maximum(-delta, 0)
    max_exact = 16
    is_small = rp < max_exact
    rpf = np.maximum(rp, 1).astype(np.float32)
    val = np.log(rpf / np.float32(max_exact)) / np.float32(math.log(128 / 16)) * 16
    rp_large = max_exact + val.astype(np.int32)
    rp_large = np.minimum(rp_large, 31)
    return np.where(is_small, rp, rp_large).astype(np.int64)


def _wslice_qkv(Wfull, cols):
    A = np.ascontiguousarray(Wfull[:, cols])
    A = A.reshape(KC, 128, 2, 256).transpose(2, 1, 0, 3)
    return np.ascontiguousarray(A, dtype=np.float32)


def _wslice_o(Wfull, rows):
    A = np.ascontiguousarray(Wfull[rows, :])
    A = A.reshape(2, 2, 128, 16, 128).transpose(0, 2, 1, 3, 4)
    return np.ascontiguousarray(A, dtype=np.float32)


_CACHE = {}


def make_in_maps(L0_attn_a_hidden_states, L0_attn_a_Wq, L0_attn_a_Wk, L0_attn_a_Wv,
                 L0_attn_a_Wo, L0_attn_a_rel_bias, L0_attn_b_Wq, L0_attn_b_Wk,
                 L0_attn_b_Wv, L0_attn_b_Wo, L0_ffn_gate_w, L0_ffn_shared_gate_w,
                 L0_ffn_shared_gate_up_w, L0_ffn_shared_down_w, L0_ffn_w13, L0_ffn_w2):
    x = np.asarray(L0_attn_a_hidden_states, dtype=np.float32)
    rel = np.asarray(L0_attn_a_rel_bias, dtype=np.float32)

    ident = np.eye(128, dtype=np.float32)
    gw = np.zeros((KC, 128, 9), dtype=np.float32)
    sgw = np.asarray(L0_ffn_shared_gate_w, dtype=np.float32)
    gtw = np.asarray(L0_ffn_gate_w, dtype=np.float32)
    kidx = np.arange(D).reshape(KC, 128)
    gw[:, :, 0] = sgw[0][kidx]
    for e_ in range(NE):
        gw[:, :, 1 + e_] = gtw[e_][kidx]
    Wgu = np.asarray(L0_ffn_shared_gate_up_w, dtype=np.float32)
    gu = np.empty((44, 128, 2, KC, 128), dtype=np.float32)
    WguR = Wgu.reshape(2, 44, 128, KC, 128)
    gu[:] = WguR.transpose(1, 4, 0, 3, 2)
    gu = np.ascontiguousarray(gu)
    W13 = np.asarray(L0_ffn_w13, dtype=np.float32)
    w13R = W13.reshape(NE, 2, 11, 128, KC, 128)
    w13 = np.ascontiguousarray(w13R.transpose(0, 2, 5, 1, 4, 3))
    Wdn = np.asarray(L0_ffn_shared_down_w, dtype=np.float32)
    dnR = Wdn.reshape(16, 128, 4, 11, 128)
    dn = np.ascontiguousarray(dnR.transpose(2, 0, 4, 3, 1))
    W2 = np.asarray(L0_ffn_w2, dtype=np.float32)
    w2R = W2.reshape(NE, 16, 128, 11, 128)
    w2 = np.ascontiguousarray(w2R.transpose(0, 1, 4, 3, 2))

    waq_f = np.asarray(L0_attn_a_Wq, dtype=np.float32)
    wak_f = np.asarray(L0_attn_a_Wk, dtype=np.float32)
    wav_f = np.asarray(L0_attn_a_Wv, dtype=np.float32)
    wao_f = np.asarray(L0_attn_a_Wo, dtype=np.float32)
    wbq_f = np.asarray(L0_attn_b_Wq, dtype=np.float32)
    wbk_f = np.asarray(L0_attn_b_Wk, dtype=np.float32)
    wbv_f = np.asarray(L0_attn_b_Wv, dtype=np.float32)
    wbo_f = np.asarray(L0_attn_b_Wo, dtype=np.float32)

    deltas = np.arange(-639, 512)
    buckets = _bucket_rp(deltas)
    kk = np.arange(128)[:, None]
    qq = np.arange(512)[None, :]

    maskb = np.zeros((4, 128, 512), dtype=np.float32)
    for d_ in range(4):
        kg = d_ * 128 + np.arange(128)[:, None]
        qg = np.arange(512)[None, :]
        maskb[d_] = (qg >= kg).astype(np.float32)

    in_maps = []
    for c in range(8):
        g = c % 4
        cols = slice(g * 512, g * 512 + 512)
        my_heads_a = range(g * 8, g * 8 + 8)
        bias_tiles = np.empty((8, 5, 128, 512), dtype=np.float32)
        for hi, h in enumerate(my_heads_a):
            tvec = rel[buckets, h]
            for d_ in range(5):
                off = (d_ - 1) * 128
                idx = off + kk - qq + 639
                bias_tiles[hi, d_] = tvec[idx]
        cabv = np.concatenate([rel[0, list(my_heads_a)],
                               rel[31, list(my_heads_a)]]).astype(np.float32)[None, :]
        rbxv = np.ascontiguousarray(rel[:, list(my_heads_a)].T)
        xs = np.ascontiguousarray(x[c // 4, g * 1024:(g + 1) * 1024, :].T)
        in_maps.append({
            "x0": xs,
            "waq": _wslice_qkv(waq_f, cols), "wak": _wslice_qkv(wak_f, cols),
            "wav": _wslice_qkv(wav_f, cols), "wao": _wslice_o(wao_f, cols),
            "wbq": _wslice_qkv(wbq_f, cols), "wbk": _wslice_qkv(wbk_f, cols),
            "wbv": _wslice_qkv(wbv_f, cols), "wbo": _wslice_o(wbo_f, cols),
            "bias_a": bias_tiles, "cab": cabv, "rbx": rbxv,
            "maskb": maskb, "ident": ident, "gwT": gw,
            "wgu_sh": gu, "w13s": w13, "wdn_sh": dn, "w2s": w2,
        })
    return in_maps


def assemble(results):
    out = np.empty((2, 4096, D), dtype=np.float32)
    for c in range(8):
        g = c % 4
        out[c // 4, g * 1024:(g + 1) * 1024, :] = results[c]["y"].T
    return out


def kernel(**inputs):
    in_maps = make_in_maps(**inputs)
    key = "nc4"
    if key not in _CACHE:
        _CACHE[key] = build_program(stages=4)
    nc = _CACHE[key]
    res = run_bass_kernel_spmd(nc, in_maps, core_ids=list(range(8)))
    return assemble(res.results)


# version 10 (final)
# speedup vs baseline: 1.0707x; 1.0707x over previous
import math
from contextlib import ExitStack

import numpy as np
import concourse.bass as bass
import concourse.mybir as mybir
from concourse.bass_utils import run_bass_kernel_spmd

F32 = mybir.dt.float32
BF16 = mybir.dt.bfloat16
F16 = mybir.dt.float16
AF = mybir.ActivationFunctionType
ALU = mybir.AluOpType
AX = mybir.AxisListType.X

D = 2048
KC = 16
NT = 1024
NE = 8
RG = [[0, 1, 2, 3], [4, 5, 6, 7]]
ENGINES = ("pe", "act", "dve", "sp", "gp")
ISQ_DB = 1.0 / math.sqrt(128.0)


class PB:
    def __init__(self, nc, sems):
        self.nc = nc
        self.sems = sems
        self.q = {k: [] for k in ENGINES}
        self.cnt = {k: 0 for k in sems}
        self.waited = {e: {} for e in ENGINES}

    def ev(self, sem):
        return (sem, self.cnt[sem])

    DMA_SEMS = ("sx", "sx2", "sxc0", "sxc1", "sg0", "sg1", "sg2", "sy0", "sy1",
                "swr", "syo", "gw", "gb", "gg0", "gg1", "gd0", "gd1",
                "gy0", "gy1")

    def bar(self):
        return [self.ev(s) for s in ("pe", "act", "dve") + self.DMA_SEMS]

    def on(self, eng, waits, fn, inc=None, amt=1):
        wl = []
        for w in (waits or ()):
            if w is None:
                continue
            s, v = w
            if v <= 0 or self.waited[eng].get(s, 0) >= v:
                continue
            self.waited[eng][s] = v
            wl.append((s, v))
        evt = None
        if inc is not None:
            self.cnt[inc] += (amt or 1)
            evt = (inc, self.cnt[inc])
        sems = self.sems

        def closure(e, wl=tuple(wl), fn=fn, inc=inc, amt=amt):
            for s, v in wl:
                e.wait_ge(sems[s], v)
            last = fn(e)
            if inc is not None:
                if amt is None:
                    last.then_inc(sems[inc])
                else:
                    last.then_inc(sems[inc], amt)

        self.q[eng].append(closure)
        return evt


def build_program(stages=4, dbg=False):
    nc = bass.Bass("TRN2", target_bir_lowering=False, debug=False)

    def din(name, shape):
        return nc.dram_tensor(name, shape, F32, kind="ExternalInput").ap()

    x0 = din("x0", [D, NT])
    w_attn = {
        "a": {k: din("wa" + k, [2, 128, KC, 256]) for k in "qkv"}
        | {"o": din("wao", [2, 128, 2, 16, 128])},
        "b": {k: din("wb" + k, [2, 128, KC, 256]) for k in "qkv"}
        | {"o": din("wbo", [2, 128, 2, 16, 128])},
    }
    bias_a = din("bias_a", [8, 5, 128, 512])
    cab_in = din("cab", [1, 16])
    rbx_in = din("rbx", [8, 32])
    maskb_in = din("maskb", [4, 128, 512])
    ident_in = din("ident", [128, 128])
    gw_in = din("gwT", [KC, 128, 9])
    wgu_sh = din("wgu_sh", [44, 128, 2, KC, 128])
    w13s = din("w13s", [NE, 11, 128, 2, KC, 128])
    wdn_sh = din("wdn_sh", [4, 16, 128, 11, 128])
    w2s = din("w2s", [NE, 16, 128, 11, 128])

    y_out = nc.dram_tensor("y", [D, NT], F32, kind="ExternalOutput").ap()
    dbg_t = {}
    if dbg:
        for nm in ("dq", "dk", "dq1", "dk1", "do0", "do1"):
            dbg_t[nm] = nc.dram_tensor(nm, [128, 4096], F16, kind="ExternalOutput").ap()
        dbg_t["dv"] = nc.dram_tensor("dv", [128, 4096], BF16, kind="ExternalOutput").ap()
        dbg_t["dp"] = nc.dram_tensor("dp", [128, 1024], BF16, kind="ExternalOutput").ap()
        dbg_t["db"] = nc.dram_tensor("db", [128, 16], F32, kind="ExternalOutput").ap()
        dbg_t["dbia"] = nc.dram_tensor("dbia", [128, 2560], BF16, kind="ExternalOutput").ap()

    ag_in = [nc.dram_tensor(f"ag_in{k}", [D // 4, NT], F16).ap() for k in range(4)]
    ag_out = [nc.dram_tensor(f"ag_out{k}", [D, NT], F16).ap() for k in range(4)]
    yp = [nc.dram_tensor(f"yp{k}", [D, NT], F32).ap() for k in range(4)]
    yo = [nc.dram_tensor(f"yo{k}", [D // 4, NT], F32).ap() for k in range(4)]
    agv_k = [ag_out[k].rearrange("(r c p) t -> r c p t", c=4, p=128) for k in range(4)]
    ypv_k = [yp[k].rearrange("(r c p) t -> r c p t", c=4, p=128) for k in range(4)]

    with ExitStack() as ctx:
        sem_names = ["pe", "act", "dve", "gpc", "sx", "sx2", "sxc0", "sxc1", "sg0", "sg1", "sg2",
                     "sy0", "sy1", "swr", "syo", "gw", "gb", "gg0", "gg1",
                     "gd0", "gd1", "gy0", "gy1"] + [
                        f"ca{li}{k}" for li in range(2) for k in range(4)] + [
                        f"cr{li}{k}" for li in range(2) for k in range(4)]
        sems = {n: ctx.enter_context(nc.semaphore(n)) for n in sem_names}

        def sball(stk, name, part, cols, dt):
            return stk.enter_context(nc.sbuf_tensor(name, [part, cols], dt))

        X = sball(ctx, "X", 128, KC * 1024, F32)
        SCR = sball(ctx, "SCR", 128, 2560, F32)
        TMP = [sball(ctx, f"TMP{i}", 128, 1024, F32) for i in range(2)]
        AB = sball(ctx, "AB", 128, 2048, F16)
        RB = sball(ctx, "RB", 128, 1024, F32)
        SML = sball(ctx, "SML", 128, 192, F32)
        SMB = sball(ctx, "SMB", 128, 32, BF16)
        SMH = sball(ctx, "SMH", 128, 2, F16)
        IDT = sball(ctx, "IDT", 128, 128, F32)
        B128 = sball(ctx, "B128", 128, 16, F32)
        RBXs = sball(ctx, "RBX", 8, 32, F32)
        RBM = sball(ctx, "RBM", 1, 16, F32)

        PS = [ctx.enter_context(nc.psum_tensor(f"PS{i}", [128, 1024], F32))
              for i in range(4)]
        PA, PBm, PC, PD = PS

        p = PB(nc, sems)
        last_x_ev = [[] for _ in range(KC)]

        ONESB = SMB[:, 0:1]
        ONESH = SMH[:, 0:1]
        ONESF = SML[0:1, 0:128]
        EPS = SML[0:1, 128:129]
        MH = SML[0:1, 129:133]
        BV = SML[0:1, 136:140]
        R_NM = SML[:, 144:145]
        R_SUM = SML[:, 145:146]
        R_RS = SML[:, 146:147]
        R_M1 = SML[:, 147:148]
        R_M2 = SML[:, 148:149]
        RBRED = SML[0:8, 160:161]
        CABS = SML[0:1, 176:192]
        MROW = SCR[0:1, 0:1024]
        TROW = SCR[0:1, 1024:2048]
        NRUN = SCR[0:1, 2048:2560]

        for c in range(KC):
            p.on("sp", [], lambda e, c=c: e.dma_start(
                out=X[:, c * 1024:(c + 1) * 1024],
                in_=x0.rearrange("(c p) t -> p c t", p=128)[:, c, :]),
                inc="sx", amt=16)
        p.on("sp", [], lambda e: e.dma_start(out=IDT[:, :], in_=ident_in),
             inc="sx", amt=16)
        p.on("sp", [], lambda e: e.dma_start(out=RBXs[:, :], in_=rbx_in),
             inc="sx", amt=16)
        p.on("sp", [], lambda e: e.dma_start(out=CABS, in_=cab_in),
             inc="sx", amt=16)
        ev_ld0 = p.ev("sx")
        for c in range(KC):
            last_x_ev[c].append(ev_ld0)
        ev_consts = [
            p.on("dve", [], lambda e: e.memset(SMB[:, 0:1], 1.0), inc="dve"),
            p.on("dve", [], lambda e: e.memset(SMH[:, 0:1], 1.0), inc="dve"),
            p.on("dve", [], lambda e: e.memset(SML[0:1, 0:128], 1.0), inc="dve"),
            p.on("dve", [], lambda e: e.memset(SML[0:1, 128:129], 1e-5), inc="dve"),
        ]
        ev_rbred = p.on("dve", [ev_ld0], lambda e:
                        e.reduce_max(RBRED, RBXs[:, :], axis=AX), inc="dve")
        for h in range(8):
            p.on("sp", [ev_rbred], lambda e, h=h: e.dma_start(
                out=RBM[0:1, h:h + 1], in_=SML[h:h + 1, 160:161]),
                inc="sx2", amt=16)
        ev_rbm = p.ev("sx2")

        def ph_ln(stk, tag, dst, dst_guard=None, dst_dma=None):
            XBh = [sball(stk, f"XB{tag}{i}", 128, 512, BF16) for i in range(2)]
            SQh = [sball(stk, f"SQ{tag}{i}", 128, 512, BF16) for i in range(2)]
            entry = p.bar()
            ev_mm_prev = [None, None]
            for c in range(KC):
                deps = last_x_ev[c] + ev_consts + entry
                for h in range(2):
                    sl = h
                    xch = X[:, c * 1024 + h * 512: c * 1024 + h * 512 + 512]
                    e1 = p.on("dve", deps + [ev_mm_prev[sl]],
                              lambda e, sl=sl, xch=xch:
                              e.tensor_copy(XBh[sl][:, :], xch), inc="dve")
                    e2 = p.on("act", deps + [ev_mm_prev[sl]],
                              lambda e, sl=sl, xch=xch:
                              e.activation(SQh[sl][:, :], xch, AF.Square), inc="act")
                    st_, sp_ = c == 0, c == KC - 1
                    e3 = p.on("pe", [e1, e2], lambda e, sl=sl, h=h, st_=st_, sp_=sp_: (
                        e.matmul(PD[0:1, h * 512:h * 512 + 512], ONESB,
                                 XBh[sl][:, :], start=st_, stop=sp_),
                        e.matmul(PC[0:1, h * 512:h * 512 + 512], ONESB,
                                 SQh[sl][:, :], start=st_, stop=sp_))[-1],
                        inc="pe")
                    ev_mm_prev[sl] = e3
            ev_stats = p.ev("pe")
            e1 = p.on("dve", [ev_stats], lambda e:
                      e.tensor_scalar_mul(MROW, PD[0:1, 0:1024], 1.0 / D), inc="dve")
            e2 = p.on("dve", [e1], lambda e:
                      e.tensor_tensor(TROW, MROW, MROW, ALU.mult), inc="dve")
            e3 = p.on("dve", [e2], lambda e:
                      e.scalar_tensor_tensor(TROW, PC[0:1, 0:1024], 1.0 / D, TROW,
                                             ALU.mult, ALU.subtract), inc="dve")
            e4 = p.on("act", [e3], lambda e:
                      e.activation(TROW, TROW, AF.Ln, bias=EPS), inc="act")
            e5 = p.on("act", [e4], lambda e:
                      e.activation(TROW, TROW, AF.Exp, scale=-0.5), inc="act")
            e6 = p.on("dve", [e5], lambda e:
                      e.scalar_tensor_tensor(MROW, MROW, -1.0, TROW,
                                             ALU.mult, ALU.mult), inc="dve")
            e7 = p.on("pe", [e5, e6] + entry, lambda e: (
                e.matmul(PA[:, 0:512], ONESF, TROW[0:1, 0:512], start=True, stop=True),
                e.matmul(PA[:, 512:1024], ONESF, TROW[0:1, 512:1024], start=True, stop=True),
                e.matmul(PBm[:, 0:512], ONESF, MROW[0:1, 0:512], start=True, stop=True),
                e.matmul(PBm[:, 512:1024], ONESF, MROW[0:1, 512:1024], start=True, stop=True))[-1],
                inc="pe")
            e8 = p.on("act", [e7], lambda e:
                      e.activation(AB[:, 0:1024], PA[:, :], AF.Copy), inc="act")
            e9 = p.on("act", [e8], lambda e:
                      e.activation(AB[:, 1024:2048], PBm[:, :], AF.Copy), inc="act")
            ev_chunks = []
            ev_tmp_prev = [None, None]
            for c in range(KC):
                xc = X[:, c * 1024:(c + 1) * 1024]
                sl = c % 2
                g = list(dst_guard(c)) if dst_guard else []
                t1 = p.on("gp", [e8, e9, ev_tmp_prev[sl]] + last_x_ev[c] + entry,
                          lambda e, xc=xc, sl=sl:
                          e.tensor_tensor(TMP[sl][:, :], xc, AB[:, 0:1024], ALU.mult),
                          inc="gpc", amt=1)
                dstap = dst(c)
                t2 = p.on("dve", [t1] + g, lambda e, sl=sl, dstap=dstap:
                          e.tensor_tensor(dstap, TMP[sl][:, :], AB[:, 1024:2048], ALU.add),
                          inc="dve")
                ev_tmp_prev[sl] = t2
                ev_chunks.append(dst_dma(c, t2) if dst_dma else t2)
            return ev_chunks, [e1, e3, e8, e9]

        def ph_attn(stk, var, li):
            a = var == "a"
            HD = 64 if a else 128
            NHB = 4 if a else 2
            VS = 65 if a else 128
            agsem, rssem = ("cc0", "cc1") if li == 0 else ("cc2", "cc3")
            W = w_attn[var]

            QT = [sball(stk, f"QT{var}{i}", 128, 4096, F16) for i in range(2)]
            KT = [sball(stk, f"KT{var}{i}", 128, 4096, F16) for i in range(2)]
            V4 = sball(stk, f"V4{var}", 128, 32 * VS * NHB, BF16)
            OUT = [sball(stk, f"OUT{var}{i}", 128, 4096, F16) for i in range(2)]
            PT = [sball(stk, f"PT{var}{i}", 128, 1024, BF16) for i in range(3)]
            WAB = sball(stk, f"WAB{var}", 128, 8192, F16)
            BIA = sball(stk, f"BIA{var}", 128, 2560 if a else 2048, BF16)
            XGT = [sball(stk, f"XGT{var}{i}", 128, 1024, F16) for i in range(3)]
            XNC = [sball(stk, f"XNC{var}{i}", 128, 1024, F16) for i in range(2)]
            MIX = sball(stk, f"MIX{var}", 128, 512, F32) if a else None
            YST = [sball(stk, f"YST{var}{i}", 128, 512, F32) for i in range(2)]
            SQB = [sball(stk, f"SQB{var}{i}", 128, 512, F16) for i in range(2)]
            V4r = V4.ap().rearrange("p (h r) -> p h r", h=NHB)

            ev_xnc_dma = [None]
            ev_ag = [None] * 4
            ag_guard = [p.ev(s) for s in ("sxc0", "sxc1", "sg0", "sg1", "sg2")]

            ev_xnc_dma2 = [None, None]

            def xnc_guard(c):
                return [ev_xnc_dma2[c % 2]]

            def xnc_dma(c, ev):
                w = [ev]
                if li == 1:
                    w.append((f"ca0{c // 4}", 1))
                out = ag_in[c // 4].rearrange("(c p) t -> p c t", p=128)[:, c % 4, :]
                evd = p.on("sp", w, lambda e, out=out, c=c:
                           e.dma_start(out=out, in_=XNC[c % 2][:, :]), inc=f"sxc{c % 2}", amt=16)
                ev_xnc_dma2[c % 2] = evd
                if c % 4 == 3:
                    k = c // 4
                    p.on("gp", [ev_xnc_dma2[0], ev_xnc_dma2[1]] + ag_guard,
                         lambda e, k=k: e.collective_compute(
                        "AllGather", ALU.bypass, replica_groups=RG,
                        ins=[ag_in[k]], outs=[ag_out[k]]), inc=f"ca{li}{k}", amt=None)
                    ev_ag[k] = (f"ca{li}{k}", 1)
                return evd

            phase_bar = p.bar()
            ln_evs, ln_guard = ph_ln(stk, f"A{var}", lambda c: XNC[c % 2][:, :],
                                     xnc_guard, xnc_dma)

            ev_masks = None
            if not a:
                ev_masks = p.on("gp", p.bar(), lambda e: e.dma_start(
                    out=BIA.ap().rearrange("p (d t) -> p d t", d=4),
                    in_=maskb_in.rearrange("d p t -> p d t")),
                    inc="gb", amt=16)

            ev_y_hb_done = None
            ev_nm_prev = None
            ev_yst = [None, None]
            ev_bc_read = None
            ev_hb_prev_dve = []
            for hb in range(2):
                entry_pe = p.ev("pe")
                wdeps = [entry_pe]
                p.on("gp", wdeps, lambda e, hb=hb:
                     e.dma_start(out=WAB[:, 0:4096], in_=W["q"][hb]),
                     inc="gw", amt=16)
                evw = p.on("gp", [], lambda e, hb=hb:
                           e.dma_start(out=WAB[:, 4096:8192], in_=W["k"][hb]),
                           inc="gw", amt=16)
                qk_mm = {}
                ev_evac_last = []
                first = True
                for r in range(4):
                    for kc in range(KC):
                        i = r * KC + kc
                        sl = i % 3
                        srcx = agv_k[kc // 4][r, kc % 4, :, :]
                        evx = p.on("sp", [ev_ag[kc // 4], qk_mm.get(i - 3), entry_pe],
                                   lambda e, srcx=srcx, sl=sl:
                                   e.dma_start(out=XGT[sl][:, :], in_=srcx),
                                   inc=f"sg{sl}", amt=16)
                        st_, sp_ = kc == 0, kc == KC - 1
                        deps = [evx, evw]
                        if kc == 0 and ev_evac_last:
                            deps += ev_evac_last
                        if first:
                            deps += ln_guard + list(ev_hb_prev_dve)
                            first = False

                        def qkmm(e, sl=sl, kc=kc, st_=st_, sp_=sp_):
                            last = None
                            for half in range(2):
                                rhs = XGT[sl][:, half * 512:half * 512 + 512]
                                qp = [PA, PBm][half]
                                kp = [PC, PD][half]
                                for pi in range(2):
                                    last = e.matmul(qp[:, pi * 512:pi * 512 + 512],
                                                    WAB[:, kc * 256 + pi * 128:kc * 256 + pi * 128 + 128],
                                                    rhs, start=st_, stop=sp_)
                                    last = e.matmul(kp[:, pi * 512:pi * 512 + 512],
                                                    WAB[:, 4096 + kc * 256 + pi * 128:4096 + kc * 256 + pi * 128 + 128],
                                                    rhs, start=st_, stop=sp_)
                            return last
                        qk_mm[i] = p.on("pe", deps, qkmm, inc="pe")
                    eW = qk_mm[r * KC + KC - 1]
                    evl = []
                    for half in range(2):
                        tc = 2 * r + half
                        qp = [PA, PBm][half]
                        kp = [PC, PD][half]
                        for pi in range(2):
                            evl.append(p.on("act", [eW], lambda e, pi=pi, tc=tc, qp=qp:
                                            e.activation(QT[pi][:, tc * 512:tc * 512 + 512],
                                                         qp[:, pi * 512:pi * 512 + 512],
                                                         AF.Copy), inc="act"))
                            evl.append(p.on("dve", [eW], lambda e, pi=pi, tc=tc, kp=kp:
                                            e.tensor_copy(KT[pi][:, tc * 512:tc * 512 + 512],
                                                          kp[:, pi * 512:pi * 512 + 512]),
                                            inc="dve"))
                    ev_evac_last = evl
                ev_qk_done_pe = p.ev("pe")
                ev_qk_evacs = p.ev("dve")
                ev_qk_evacs_act = p.ev("act")

                evwv = p.on("gp", [ev_qk_done_pe], lambda e, hb=hb:
                            e.dma_start(out=WAB[:, 0:4096], in_=W["v"][hb]),
                            inc="gw", amt=16)
                ev_vset = None
                if a:
                    ev_vset = p.on("dve", [p.ev("pe")], lambda e:
                                   e.memset(V4[:, :], 1.0), inc="dve")
                PSQ0 = [PC[:, 0:256], PC[:, 512:768], PD[:, 0:256], PD[:, 512:768]]
                PSQ1 = [PA[:, 0:256], PA[:, 512:768], PBm[:, 0:256], PBm[:, 512:768]]
                v_mm = {}
                ev_v_evac_last = [None, None]
                for r in range(4):
                    for kc in range(KC):
                        i = r * KC + kc
                        sl = i % 3
                        srcx = agv_k[kc // 4][r, kc % 4, :, :]
                        evx = p.on("sp", [v_mm.get(i - 3), ev_qk_done_pe, ev_qk_evacs],
                                   lambda e, srcx=srcx, sl=sl:
                                   e.dma_start(out=XGT[sl][:, :], in_=srcx),
                                   inc=f"sg{sl}", amt=16)
                        st_, sp_ = kc == 0, kc == KC - 1
                        deps = [evx, evwv, ev_vset, ev_qk_evacs, ev_qk_evacs_act]
                        if kc == 0:
                            deps += ev_v_evac_last

                        def vmm2(e, sl=sl, kc=kc, st_=st_, sp_=sp_):
                            last = None
                            for half in range(2):
                                psq = [PSQ0, PSQ1][half]
                                for sub in range(4):
                                    last = e.matmul(psq[sub],
                                                    XGT[sl][:, half * 512 + sub * 128:half * 512 + sub * 128 + 128],
                                                    WAB[:, kc * 256:kc * 256 + 256],
                                                    start=st_, stop=sp_)
                            return last
                        v_mm[i] = p.on("pe", deps, vmm2, inc="pe")
                    eW = v_mm[r * KC + KC - 1]
                    for half in range(2):
                        tc = 2 * r + half
                        psq = [PSQ0, PSQ1][half]
                        for sub in range(4):
                            kt = tc * 4 + sub
                            dstv = V4r[:, :, kt * VS: kt * VS + HD]
                            srcv = psq[sub].rearrange("p (h d) -> p h d", h=NHB)
                            if sub % 2 == 0:
                                ev_v_evac_last[0] = p.on("dve", [eW, ev_vset],
                                                         lambda e, dstv=dstv, srcv=srcv:
                                                         e.tensor_copy(dstv, srcv), inc="dve")
                            else:
                                ev_v_evac_last[1] = p.on("act", [eW, ev_vset],
                                                         lambda e, dstv=dstv, srcv=srcv:
                                                         e.activation(dstv, srcv, AF.Copy),
                                                         inc="act")
                ev_vdone_pe = p.ev("pe")
                ev_vdone = p.ev("dve")
                ev_vdone_act = p.ev("act")

                ev_norm_read = None
                s2_guard = {i: [] for i in range(6)}
                ev_bias_ready_h = {}
                ev_biat_h = {}
                ev_sq_mm_prev = [None, None]
                ev_run_prev = ev_nm_prev
                e3chain = None
                ev_e4_prev = None
                for hh in range(NHB):
                    pi, hoff = (hh // 2, (hh % 2) * 64) if a else (hh, 0)
                    hglob = hb * NHB + hh
                    ev_nm = []
                    for ti, Tsrc in enumerate((QT, KT)):
                        for cc in range(8):
                            sl = cc % 2
                            srcap = Tsrc[pi][hoff:hoff + HD, cc * 512:cc * 512 + 512]
                            esq = p.on("act", [ev_qk_evacs, ev_qk_evacs_act,
                                               ev_sq_mm_prev[sl]],
                                       lambda e, srcap=srcap, sl=sl:
                                       e.activation(SQB[sl][0:HD, :], srcap, AF.Square),
                                       inc="act")
                            emm = p.on("pe", [esq, ev_vdone, ev_vdone_act, ev_run_prev],
                                       lambda e, sl=sl:
                                       e.matmul(PC[0:1, 0:512], ONESH[0:HD, 0:1],
                                                SQB[sl][0:HD, :], start=True, stop=True),
                                       inc="pe")
                            ev_sq_mm_prev[sl] = emm
                            if cc == 0:
                                ev_run_prev = p.on("dve", [emm, ev_run_prev], lambda e:
                                                   e.tensor_copy(NRUN, PC[0:1, 0:512]),
                                                   inc="dve")
                            else:
                                ev_run_prev = p.on("dve", [emm, ev_run_prev], lambda e:
                                                   e.tensor_tensor(NRUN, NRUN,
                                                                   PC[0:1, 0:512], ALU.max),
                                                   inc="dve")
                        ev_nm.append(p.on("dve", [ev_run_prev], lambda e, ti=ti:
                                          e.reduce_max(MH[0:1, ti:ti + 1], NRUN, axis=AX),
                                          inc="dve"))
                        ev_run_prev = ev_nm[-1]
                    e0 = p.on("dve", ev_nm + [e3chain], lambda e:
                              e.tensor_tensor(MH[0:1, 2:3], MH[0:1, 0:1], MH[0:1, 1:2],
                                              ALU.mult), inc="dve")
                    e1 = p.on("act", [e0], lambda e:
                              e.activation(MH[0:1, 2:3], MH[0:1, 2:3], AF.Ln), inc="act")
                    sm = 1.0 if a else ISQ_DB
                    e1b = p.on("act", [e1], lambda e:
                               e.activation(MH[0:1, 2:3], MH[0:1, 2:3], AF.Exp, scale=0.5),
                               inc="act")
                    e2 = p.on("dve", [e1b], lambda e, sm=sm:
                              e.tensor_scalar_mul(MH[0:1, 2:3], MH[0:1, 2:3], sm),
                              inc="dve")
                    if a:
                        e2 = p.on("dve", [e2, ev_rbm], lambda e, hglob=hglob:
                                  e.tensor_tensor(MH[0:1, 2:3], MH[0:1, 2:3],
                                                  RBM[0:1, hglob:hglob + 1], ALU.add),
                                  inc="dve")
                    e2b = p.on("dve", [e2], lambda e:
                               e.tensor_scalar_sub(MH[0:1, 2:3], MH[0:1, 2:3], 60.0),
                               inc="dve")
                    e3 = p.on("dve", [e2b, ev_e4_prev], lambda e:
                              e.tensor_scalar_mul(BV[0:1, 0:1], MH[0:1, 2:3], -1.0),
                              inc="dve")
                    if a:
                        e3 = p.on("dve", [e3, ev_ld0], lambda e, hglob=hglob: (
                            e.scalar_tensor_tensor(BV[0:1, 1:2], MH[0:1, 2:3], -1.0,
                                                   CABS[0:1, hglob:hglob + 1],
                                                   ALU.mult, ALU.add),
                            e.scalar_tensor_tensor(BV[0:1, 2:3], MH[0:1, 2:3], -1.0,
                                                   CABS[0:1, 8 + hglob:9 + hglob],
                                                   ALU.mult, ALU.add))[-1], inc="dve")
                    e3chain = e3
                    nbv = 3 if a else 1
                    e4 = p.on("pe", [e3, ev_vdone, ev_bc_read,
                                     ev_bias_ready_h.get(hh - 1)], lambda e, nbv=nbv:
                              e.matmul(PD[:, 768:768 + nbv], ONESF, BV[0:1, 0:nbv],
                                       start=True, stop=True), inc="pe")
                    ev_bias_ready_h[hh] = p.on("act", [e4], lambda e, nbv=nbv, hh=hh:
                                               e.activation(B128[:, nbv * hh:nbv * hh + nbv],
                                                            PD[:, 768:768 + nbv],
                                                            AF.Copy), inc="act")
                    ev_e4_prev = e4
                ev_nm_prev = ev_run_prev
                ev_norm_read = ev_nm_prev
                s2_guard[4] = s2_guard[5] = [ev_bias_ready_h[NHB - 1]]


                for hh in range(NHB):
                    pi, hoff = (hh // 2, (hh % 2) * 64) if a else (hh, 0)
                    hglob = hb * NHB + hh
                    ev_bias_ready = ev_bias_ready_h[hh]
                    ev_biat = None
                    if a:
                        ev_biat = p.on("gp", [p.ev("act"), p.ev("dve")],
                                       lambda e, hglob=hglob: e.dma_start(
                                           out=BIA.ap().rearrange("p (d t) -> p d t", d=5),
                                           in_=bias_a[hglob].rearrange("d p t -> p d t")),
                                       inc="gb", amt=16)
                    bvbase = (3 if a else 1) * hh
                    for qc in range(4):
                        nj = (32 if a else 8 * qc + 8) * 2
                        ktmax = nj // 2

                        def offh(j):
                            return (j // 2) * 128 - (qc * 1024 + (j % 2) * 512)
                        if a:
                            live = list(range(nj))
                        else:
                            live = [j for j in range(nj) if offh(j) < 512]
                        fl = {0: None, 1: None}
                        ll = {0: None, 1: None}
                        for j in live:
                            if fl[j % 2] is None:
                                fl[j % 2] = j
                            ll[j % 2] = j
                        SLOT = [PA[:, 0:512], PA[:, 512:1024],
                                PBm[:, 0:512], PBm[:, 512:1024]]
                        if a:
                            SLOT = SLOT + [PD[:, 0:512], PD[:, 512:1024]]
                        NSLOT = len(SLOT)
                        ev_s = {}
                        ev_exp = {}
                        ev_av = {}
                        exp_read = {}

                        def emit_s(j, si):
                            kt, half = j // 2, j % 2
                            sb_ = SLOT[si % NSLOT]
                            lhs = KT[pi][hoff:hoff + HD, kt * 128:kt * 128 + 128]
                            rhs = QT[pi][hoff:hoff + HD,
                                         qc * 1024 + half * 512: qc * 1024 + half * 512 + 512]
                            deps = [ev_bias_ready, exp_read.get(si - NSLOT)]
                            if si < NSLOT:
                                deps += s2_guard[si % NSLOT]
                            ev_s[j] = p.on("pe", deps, lambda e, lhs=lhs, rhs=rhs, sb_=sb_:
                                           e.matmul(sb_, lhs, rhs, start=True, stop=True),
                                           inc="pe")

                        def emit_exp(j, si):
                            kt, half = j // 2, j % 2
                            sb_ = SLOT[si % NSLOT]
                            pt = PT[(si % 6) // 2]
                            ptap = pt[:, (si % 2) * 512:(si % 2) * 512 + 512]
                            dep_pt = [ev_av.get(si - 6)]
                            if a:
                                off = offh(j)
                                if off in (-128, 0, 128, 256, 384):
                                    dd = (off + 128) // 128
                                    em = p.on("dve", [ev_s[j], ev_biat, exp_read.get("mix")],
                                              lambda e, sb_=sb_, dd=dd:
                                              e.tensor_tensor(MIX[:, :], sb_,
                                                              BIA[:, dd * 512:dd * 512 + 512],
                                                              ALU.add), inc="dve")
                                    exp_read[si] = em
                                    eh = p.on("act", [em] + dep_pt,
                                              lambda e, ptap=ptap, bvbase=bvbase:
                                              e.activation(ptap, MIX[:, :], AF.Exp,
                                                           bias=B128[:, bvbase:bvbase + 1]),
                                              inc="act")
                                    exp_read["mix"] = eh
                                else:
                                    bcol = 1 if off >= 512 else 2
                                    eh = p.on("act", [ev_s[j]] + dep_pt,
                                              lambda e, ptap=ptap, sb_=sb_, bcol=bcol, bvbase=bvbase:
                                              e.activation(ptap, sb_, AF.Exp,
                                                           bias=B128[:, bvbase + bcol:bvbase + bcol + 1]),
                                              inc="act")
                                    exp_read[si] = eh
                                ev_exp[j] = eh
                            else:
                                ee = p.on("act", [ev_s[j]] + dep_pt,
                                          lambda e, ptap=ptap, sb_=sb_, bvbase=bvbase:
                                          e.activation(ptap, sb_, AF.Exp,
                                                       bias=B128[:, bvbase:bvbase + 1],
                                                       scale=ISQ_DB),
                                          inc="act")
                                exp_read[si] = ee
                                off = offh(j)
                                if 0 <= off < 512:
                                    dd = off // 128
                                    ee = p.on("dve", [ee, ev_masks],
                                              lambda e, ptap=ptap, dd=dd:
                                              e.tensor_tensor(ptap, ptap,
                                                              BIA[:, dd * 512:dd * 512 + 512],
                                                              ALU.mult), inc="dve")
                                ev_exp[j] = ee

                        def emit_av(j, si):
                            kt, half = j // 2, j % 2
                            pt = PT[(si % 6) // 2]
                            ptap = pt[:, (si % 2) * 512:(si % 2) * 512 + 512]
                            vl = V4r[:, hh, kt * VS: kt * VS + (VS if a else 128)]
                            st_, sp_ = j == fl[half], j == ll[half]
                            avdeps = [ev_exp[j]]
                            if si < 2:
                                avdeps += [ev_norm_read, ev_vdone, ev_vdone_act]
                                if not a:
                                    avdeps.append(ev_bc_read)

                            def avfn(e, vl=vl, ptap=ptap, st_=st_, sp_=sp_, half=half):
                                last = e.matmul(PC[0:(65 if a else 128),
                                                   half * 512:half * 512 + 512],
                                                vl, ptap, start=st_, stop=sp_)
                                if not a:
                                    last = e.matmul(PD[0:1, half * 512:half * 512 + 512],
                                                    ONESB, ptap, start=st_, stop=sp_)
                                return last
                            ev_av[si] = p.on("pe", avdeps, avfn, inc="pe")
                        nl = len(live)
                        LE = 4
                        for si in range(min(NSLOT, nl)):
                            emit_s(live[si], si)
                            if si < LE:
                                emit_exp(live[si], si)
                        for si in range(nl + 2):
                            if si >= 2:
                                emit_av(live[si - 2], si - 2)
                            if si + NSLOT < nl:
                                emit_s(live[si + NSLOT], si + NSLOT)
                            if LE <= si + LE < nl:
                                emit_exp(live[si + LE], si + LE)
                        lsrc = PC[64:65, 0:1024] if a else PD[0:1, 0:1024]
                        er = p.on("dve", [ev_av[nl - 1], ev_av[nl - 2]],
                                  lambda e, lsrc=lsrc:
                                  e.reciprocal(RB[0:1, 0:1024], lsrc), inc="dve")
                        ebdeps = [er, ev_bc_read] + [exp_read.get(nl - k)
                                                      for k in range(1, NSLOT + 1)]
                        eb = p.on("pe", ebdeps, lambda e: (
                            e.matmul(PD[0:(64 if a else 128), 0:512], ONESF[0:1, 0:(64 if a else 128)],
                                     RB[0:1, 0:512], start=True, stop=True),
                            e.matmul(PD[0:(64 if a else 128), 512:1024], ONESF[0:1, 0:(64 if a else 128)],
                                     RB[0:1, 512:1024], start=True, stop=True))[-1],
                            inc="pe")
                        ec = p.on("act", [eb], lambda e:
                                  e.activation(RB[:, :], PD[:, :], AF.Copy), inc="act")
                        ev_bc_read = ec
                        dsto = OUT[pi][hoff:hoff + (64 if a else 128),
                                       qc * 1024:qc * 1024 + 1024]
                        eo = p.on("dve", [ec], lambda e, dsto=dsto:
                                  e.tensor_tensor(dsto, PC[0:(64 if a else 128), 0:1024],
                                                  RB[0:(64 if a else 128), 0:1024],
                                                  ALU.mult), inc="dve")
                        ev_norm_read = eo
                        for back in range(1, NSLOT + 1):
                            s2_guard[(nl - back) % NSLOT] = [exp_read.get(nl - back)]
                        if a:
                            s2_guard[4] = s2_guard[4] + [ec]
                            s2_guard[5] = s2_guard[5] + [ec]

                evwo = p.on("gp", [p.ev("pe")], lambda e, hb=hb:
                            e.dma_start(out=WAB[:, 0:4096], in_=W["o"][hb]),
                            inc="gw", amt=16)
                ev_all_dve = p.ev("dve")
                ev_all_act = p.ev("act")
                ev_ee = {}
                if hb == 1:
                    ev_y_hb0 = ev_y_hb_done
                for oc in range(16):
                    for tc in range(8):
                        i = oc * 8 + tc
                        sl = i % 2
                        yps = [PA, PBm][sl]
                        deps = [evwo, ev_all_dve, ev_all_act, ev_ee.get(i - 2)]
                        if i == 0:
                            deps += s2_guard[0] + s2_guard[1] + s2_guard[2] + s2_guard[3]
                        emm = p.on("pe", deps, lambda e, oc=oc, tc=tc, yps=yps: (
                            e.matmul(yps[:, 0:512],
                                     WAB[:, oc * 128:oc * 128 + 128],
                                     OUT[0][:, tc * 512:tc * 512 + 512],
                                     start=True, stop=False),
                            e.matmul(yps[:, 0:512],
                                     WAB[:, 2048 + oc * 128:2048 + oc * 128 + 128],
                                     OUT[1][:, tc * 512:tc * 512 + 512],
                                     start=False, stop=True))[-1],
                            inc="pe")
                        ee = p.on("act", [emm, ev_yst[sl]], lambda e, sl=sl, yps=yps:
                                  e.activation(YST[sl][:, :], yps[:, 0:512], AF.Copy),
                                  inc="act")
                        ev_ee[i] = ee
                        dst = ypv_k[oc // 4][tc // 2, oc % 4, :, (tc % 2) * 512:(tc % 2) * 512 + 512]
                        op = ALU.bypass if hb == 0 else ALU.add
                        ydeps = [ee]
                        if hb == 1 and oc == 0 and tc == 0:
                            ydeps += ev_y_hb0
                        if li == 1 and hb == 0 and tc == 0 and oc % 4 == 0:
                            ydeps.append((f"cr0{oc // 4}", 1))
                        evd = p.on("gp", ydeps, lambda e, dst=dst, sl=sl, op=op:
                                   e.dma_start(out=dst, in_=YST[sl][:, :], accum_op=op),
                                   inc=f"gy{sl}", amt=16)
                        ev_yst[sl] = evd
                    if hb == 1 and oc % 4 == 3:
                        k = oc // 4
                        p.on("gp", [p.ev("gy0"), p.ev("gy1")],
                             lambda e, k=k: e.collective_compute(
                                 "ReduceScatter", ALU.add, replica_groups=RG,
                                 ins=[yp[k]], outs=[yo[k]]), inc=f"cr{li}{k}", amt=None)
                ev_y_hb_done = [p.ev("gy0"), p.ev("gy1")]
            ev_prev = [None, None]
            for c in range(KC):
                sl = c % 2
                eld = p.on("sp", [(f"cr{li}{c // 4}", 1), ev_prev[sl]],
                           lambda e, c=c, sl=sl: e.dma_start(
                               out=TMP[sl][:, :],
                               in_=yo[c // 4].rearrange("(c p) t -> p c t", p=128)[:, c % 4, :]),
                           inc=f"sy{sl}", amt=16)
                xc = X[:, c * 1024:(c + 1) * 1024]
                ea = p.on("dve", [eld] + last_x_ev[c], lambda e, xc=xc, sl=sl:
                          e.tensor_tensor(xc, xc, TMP[sl][:, :], ALU.add), inc="dve")
                ev_prev[sl] = ea
                last_x_ev[c] = [ea]

        def ph_ffn(stk, li):
            XN = sball(stk, f"XN{li}", 128, KC * 1024, BF16)
            H = sball(stk, f"H{li}", 128, 11 * 1024, BF16)
            WG = [sball(stk, f"WG{li}{i}", 128, 2 * KC * 128, BF16) for i in range(2)]
            WD = [sball(stk, f"WD{li}{i}", 128, 11 * 128, BF16) for i in range(2)]
            WBC = sball(stk, f"WBC{li}", 128, 1024, F32)
            SGB = sball(stk, f"SGB{li}", 128, 1024, F32)
            GWT = sball(stk, f"GWT{li}", 128, 9 * KC, BF16)
            LGS = sball(stk, f"LGS{li}", 128, 80, F32)
            WSG = sball(stk, f"WSG{li}", 128, 9, F32)
            P8 = sball(stk, f"P8{li}", 128, 24, F32)
            WRO = sball(stk, f"WRO{li}", 9, 1024, F32)
            WR1 = sball(stk, f"WR1{li}", 1, 1024, F32)

            ln_evs, ln_guard = ph_ln(stk, f"F{li}", lambda c: XN[:, c * 1024:(c + 1) * 1024])
            ev_gw = p.on("gp", p.bar(), lambda e: e.dma_start(
                out=GWT.ap().rearrange("p (c n) -> p c n", c=KC),
                in_=gw_in.rearrange("c p n -> p c n")), inc="gw", amt=16)

            ev_lg = {}
            ev_lg_ev = {}
            for tb in range(8):
                poff = (tb % 2) * 512

                def rmm(e, tb=tb, poff=poff):
                    last = None
                    for kc in range(KC):
                        last = e.matmul(PD[:, poff:poff + 9],
                                        XN[:, kc * 1024 + tb * 128: kc * 1024 + tb * 128 + 128],
                                        GWT[:, kc * 9:kc * 9 + 9],
                                        start=(kc == 0), stop=(kc == KC - 1))
                    return last
                emm = p.on("pe", ln_evs + ln_guard + [ev_gw, ev_lg_ev.get(tb - 2)], rmm, inc="pe")
                ev_lg[tb] = emm
                ev_lg_ev[tb] = p.on("act", [emm], lambda e, tb=tb, poff=poff:
                                    e.activation(LGS[:, tb * 9:tb * 9 + 9],
                                                 PD[:, poff:poff + 9], AF.Copy), inc="act")
            ev_tr = []
            ev_tr_prev = None
            for tb in range(8):
                lg = LGS[:, tb * 9:tb * 9 + 9]
                e1 = p.on("dve", [ev_lg_ev[tb], ev_tr_prev], lambda e, lg=lg:
                          e.reduce_max(R_NM, lg[:, 1:9], axis=AX, negate=True), inc="dve")
                e2 = p.on("act", [e1], lambda e, lg=lg:
                          e.activation(P8[:, 0:8], lg[:, 1:9], AF.Exp, bias=R_NM), inc="act")
                e3 = p.on("dve", [e2], lambda e:
                          e.reduce_sum(R_SUM, P8[:, 0:8], axis=AX), inc="dve")
                e4 = p.on("dve", [e3], lambda e:
                          e.reciprocal(R_RS, R_SUM), inc="dve")
                e5 = p.on("dve", [e4], lambda e:
                          e.tensor_scalar_mul(P8[:, 0:8], P8[:, 0:8], R_RS), inc="dve")
                e6 = p.on("dve", [e5], lambda e:
                          e.reduce_max(R_M1, P8[:, 0:8], axis=AX), inc="dve")
                e7 = p.on("dve", [e6], lambda e:
                          e.tensor_single_scalar(P8[:, 8:16], P8[:, 0:8], R_M1, ALU.is_ge),
                          inc="dve")
                e8 = p.on("dve", [e7], lambda e:
                          e.scalar_tensor_tensor(P8[:, 16:24], P8[:, 8:16], -1e30,
                                                 P8[:, 0:8], ALU.mult, ALU.add), inc="dve")
                e9 = p.on("dve", [e8], lambda e:
                          e.reduce_max(R_M2, P8[:, 16:24], axis=AX), inc="dve")
                e10 = p.on("dve", [e9], lambda e:
                           e.tensor_single_scalar(P8[:, 8:16], P8[:, 0:8], R_M2, ALU.is_ge),
                           inc="dve")
                e11 = p.on("dve", [e10], lambda e:
                           e.tensor_tensor(WSG[:, 0:8], P8[:, 0:8], P8[:, 8:16], ALU.mult),
                           inc="dve")
                e12 = p.on("act", [e11], lambda e, lg=lg:
                           e.activation(WSG[:, 8:9], lg[:, 0:1], AF.Sigmoid), inc="act")
                etr = p.on("pe", [e11, e12, ev_ld0], lambda e, tb=tb:
                           e.transpose(PC[0:9, tb * 128:tb * 128 + 128], WSG[:, :],
                                       IDT[:, :]), inc="pe")
                ev_tr_prev = etr
                ev_tr.append(etr)
            ewro = p.on("dve", ev_tr, lambda e:
                        e.tensor_copy(WRO[:, :], PC[0:9, 0:1024]), inc="dve")
            esg0 = p.on("sp", [ewro], lambda e:
                        e.dma_start(out=WR1[0:1, :], in_=WRO[8:9, :]), inc="swr", amt=16)
            esg1 = p.on("pe", [esg0], lambda e: (
                e.matmul(PD[:, 0:512], ONESF, WR1[0:1, 0:512], start=True, stop=True),
                e.matmul(PD[:, 512:1024], ONESF, WR1[0:1, 512:1024], start=True, stop=True))[-1],
                inc="pe")
            esg2 = p.on("act", [esg1], lambda e:
                        e.activation(SGB[:, :], PD[:, :], AF.Copy), inc="act")

            branches = [("sh", j) for j in range(4)] + [("ex", k) for k in range(NE)]
            ev_wr1_mm = esg1
            ev_wbc_rd = None
            wg_used = [None, None]
            wd_used = [None, None]
            silu_rd = [None, None]
            tt1_rd = [None, None]
            tmp_used = [None, None]
            dn_tt = [None, None]
            ev_h_rd_pe = None
            ev_pd_rd = esg2
            for kind, idx in branches:
                if kind == "ex":
                    ex0 = p.on("sp", [ewro, ev_wr1_mm], lambda e, idx=idx:
                               e.dma_start(out=WR1[0:1, :], in_=WRO[idx:idx + 1, :]),
                               inc="swr", amt=16)
                    ex1 = p.on("pe", [ex0, ev_pd_rd, dn_tt[1]], lambda e: (
                        e.matmul(PD[:, 0:512], ONESF, WR1[0:1, 0:512], start=True, stop=True),
                        e.matmul(PD[:, 512:1024], ONESF, WR1[0:1, 512:1024], start=True, stop=True))[-1],
                        inc="pe")
                    ev_wr1_mm = ex1
                    scb = p.on("act", [ex1, ev_wbc_rd], lambda e:
                               e.activation(WBC[:, :], PD[:, :], AF.Copy), inc="act")
                    ev_pd_rd = scb
                    bc, ev_bc = WBC, scb
                else:
                    bc, ev_bc = SGB, esg2
                ev_h_new = []
                for op in range(11):
                    wsl = op % 2
                    src = (wgu_sh[idx * 11 + op] if kind == "sh" else w13s[idx, op])
                    evw = p.on("gp", [wg_used[wsl]], lambda e, src=src, wsl=wsl:
                               e.dma_start(out=WG[wsl].ap().rearrange(
                                   "p (s c m) -> p s c m", s=2, c=KC), in_=src),
                               inc=f"gg{wsl}", amt=16)
                    ev_s = {}
                    for s, dstp in ((0, PA), (1, PBm)):
                        guard = silu_rd[0] if s == 0 else tt1_rd[0]

                        def mmfn(e, wsl=wsl, s=s, dstp=dstp):
                            last = None
                            for kc in range(KC):
                                for half in range(2):
                                    last = e.matmul(
                                        dstp[:, half * 512:half * 512 + 512],
                                        WG[wsl][:, (s * KC + kc) * 128:(s * KC + kc) * 128 + 128],
                                        XN[:, kc * 1024 + half * 512: kc * 1024 + half * 512 + 512],
                                        start=(kc == 0), stop=(kc == KC - 1))
                            return last
                        ev_s[s] = p.on("pe", ln_evs + ln_guard + [evw, guard], mmfn, inc="pe")
                    wg_used[wsl] = ev_s[1]
                    sl = op % 2
                    es1 = p.on("act", [ev_s[0], tmp_used[sl]], lambda e, sl=sl:
                               e.activation(TMP[sl][:, :], PA[:, :], AF.Silu), inc="act")
                    silu_rd[0] = es1
                    es2 = p.on("dve", [es1, ev_s[1]], lambda e, sl=sl:
                               e.tensor_tensor(TMP[sl][:, :], TMP[sl][:, :], PBm[:, :],
                                               ALU.mult), inc="dve")
                    tt1_rd[0] = es2
                    es3 = p.on("dve", [es2, ev_bc, ev_h_rd_pe], lambda e, sl=sl, op=op, bc=bc:
                               e.tensor_tensor(H[:, op * 1024:op * 1024 + 1024],
                                               TMP[sl][:, :], bc[:, :], ALU.mult),
                               inc="dve")
                    tmp_used[sl] = es3
                    ev_h_new.append(es3)
                ev_wbc_rd = ev_h_new[-1]
                for oc in range(16):
                    wsl = oc % 2
                    src = (wdn_sh[idx, oc] if kind == "sh" else w2s[idx, oc])
                    evw = p.on("gp", [wd_used[wsl]], lambda e, src=src, wsl=wsl:
                               e.dma_start(out=WD[wsl].ap().rearrange(
                                   "p (c m) -> p c m", c=11), in_=src),
                               inc=f"gd{wsl}", amt=16)
                    dps = [PC, PD][oc % 2]
                    dndeps = [evw, dn_tt[oc % 2]] + ev_h_new
                    if oc % 2 == 1:
                        dndeps.append(ev_pd_rd)

                    def dnmm(e, wsl=wsl, dps=dps):
                        last = None
                        for jc in range(11):
                            for half in range(2):
                                last = e.matmul(
                                    dps[:, half * 512:half * 512 + 512],
                                    WD[wsl][:, jc * 128:jc * 128 + 128],
                                    H[:, jc * 1024 + half * 512: jc * 1024 + half * 512 + 512],
                                    start=(jc == 0), stop=(jc == 10))
                            pass
                        return last
                    emm = p.on("pe", dndeps, dnmm, inc="pe")
                    wd_used[wsl] = emm
                    xc = X[:, oc * 1024:(oc + 1) * 1024]
                    ea = p.on("dve", [emm] + last_x_ev[oc], lambda e, xc=xc, dps=dps:
                              e.tensor_tensor(xc, xc, dps[:, :], ALU.add), inc="dve")
                    dn_tt[oc % 2] = ea
                    last_x_ev[oc] = [ea]
                ev_h_rd_pe = wd_used[1]

        if stages >= 1:
            with ExitStack() as s1:
                ph_attn(s1, "a", 0)
        if stages >= 2:
            with ExitStack() as s2:
                ph_ffn(s2, 0)
        if stages >= 3:
            with ExitStack() as s3:
                ph_attn(s3, "b", 1)
        if stages >= 4:
            with ExitStack() as s4:
                ph_ffn(s4, 1)

        for c in range(KC):
            p.on("sp", last_x_ev[c], lambda e, c=c: e.dma_start(
                out=y_out.rearrange("(c p) t -> p c t", p=128)[:, c, :],
                in_=X[:, c * 1024:(c + 1) * 1024]), inc="syo", amt=16)
        ev_final = p.ev("syo")
        p.on("sp", [ev_final], lambda e: None)

        with nc.Block() as block:
            block.tensor(lambda e: [f(e) for f in p.q["pe"]])
            block.scalar(lambda e: [f(e) for f in p.q["act"]])
            block.vector(lambda e: [f(e) for f in p.q["dve"]])
            block.sync(lambda e: [f(e) for f in p.q["sp"]])
            block.gpsimd(lambda e: [f(e) for f in p.q["gp"]])

    return nc



def _bucket_rp(delta):
    delta = np.asarray(delta, dtype=np.int64)
    rp = np.maximum(-delta, 0)
    max_exact = 16
    is_small = rp < max_exact
    rpf = np.maximum(rp, 1).astype(np.float32)
    val = np.log(rpf / np.float32(max_exact)) / np.float32(math.log(128 / 16)) * 16
    rp_large = max_exact + val.astype(np.int32)
    rp_large = np.minimum(rp_large, 31)
    return np.where(is_small, rp, rp_large).astype(np.int64)


def _wslice_qkv(Wfull, cols):
    A = np.ascontiguousarray(Wfull[:, cols])
    A = A.reshape(KC, 128, 2, 256).transpose(2, 1, 0, 3)
    return np.ascontiguousarray(A, dtype=np.float32)


def _wslice_o(Wfull, rows):
    A = np.ascontiguousarray(Wfull[rows, :])
    A = A.reshape(2, 2, 128, 16, 128).transpose(0, 2, 1, 3, 4)
    return np.ascontiguousarray(A, dtype=np.float32)


_CACHE = {}


def make_in_maps(L0_attn_a_hidden_states, L0_attn_a_Wq, L0_attn_a_Wk, L0_attn_a_Wv,
                 L0_attn_a_Wo, L0_attn_a_rel_bias, L0_attn_b_Wq, L0_attn_b_Wk,
                 L0_attn_b_Wv, L0_attn_b_Wo, L0_ffn_gate_w, L0_ffn_shared_gate_w,
                 L0_ffn_shared_gate_up_w, L0_ffn_shared_down_w, L0_ffn_w13, L0_ffn_w2):
    x = np.asarray(L0_attn_a_hidden_states, dtype=np.float32)
    rel = np.asarray(L0_attn_a_rel_bias, dtype=np.float32)

    ident = np.eye(128, dtype=np.float32)
    gw = np.zeros((KC, 128, 9), dtype=np.float32)
    sgw = np.asarray(L0_ffn_shared_gate_w, dtype=np.float32)
    gtw = np.asarray(L0_ffn_gate_w, dtype=np.float32)
    kidx = np.arange(D).reshape(KC, 128)
    gw[:, :, 0] = sgw[0][kidx]
    for e_ in range(NE):
        gw[:, :, 1 + e_] = gtw[e_][kidx]
    Wgu = np.asarray(L0_ffn_shared_gate_up_w, dtype=np.float32)
    gu = np.empty((44, 128, 2, KC, 128), dtype=np.float32)
    WguR = Wgu.reshape(2, 44, 128, KC, 128)
    gu[:] = WguR.transpose(1, 4, 0, 3, 2)
    gu = np.ascontiguousarray(gu)
    W13 = np.asarray(L0_ffn_w13, dtype=np.float32)
    w13R = W13.reshape(NE, 2, 11, 128, KC, 128)
    w13 = np.ascontiguousarray(w13R.transpose(0, 2, 5, 1, 4, 3))
    Wdn = np.asarray(L0_ffn_shared_down_w, dtype=np.float32)
    dnR = Wdn.reshape(16, 128, 4, 11, 128)
    dn = np.ascontiguousarray(dnR.transpose(2, 0, 4, 3, 1))
    W2 = np.asarray(L0_ffn_w2, dtype=np.float32)
    w2R = W2.reshape(NE, 16, 128, 11, 128)
    w2 = np.ascontiguousarray(w2R.transpose(0, 1, 4, 3, 2))

    waq_f = np.asarray(L0_attn_a_Wq, dtype=np.float32)
    wak_f = np.asarray(L0_attn_a_Wk, dtype=np.float32)
    wav_f = np.asarray(L0_attn_a_Wv, dtype=np.float32)
    wao_f = np.asarray(L0_attn_a_Wo, dtype=np.float32)
    wbq_f = np.asarray(L0_attn_b_Wq, dtype=np.float32)
    wbk_f = np.asarray(L0_attn_b_Wk, dtype=np.float32)
    wbv_f = np.asarray(L0_attn_b_Wv, dtype=np.float32)
    wbo_f = np.asarray(L0_attn_b_Wo, dtype=np.float32)

    deltas = np.arange(-639, 512)
    buckets = _bucket_rp(deltas)
    kk = np.arange(128)[:, None]
    qq = np.arange(512)[None, :]

    maskb = np.zeros((4, 128, 512), dtype=np.float32)
    for d_ in range(4):
        kg = d_ * 128 + np.arange(128)[:, None]
        qg = np.arange(512)[None, :]
        maskb[d_] = (qg >= kg).astype(np.float32)

    in_maps = []
    for c in range(8):
        g = c % 4
        cols = slice(g * 512, g * 512 + 512)
        my_heads_a = range(g * 8, g * 8 + 8)
        bias_tiles = np.empty((8, 5, 128, 512), dtype=np.float32)
        for hi, h in enumerate(my_heads_a):
            tvec = rel[buckets, h]
            for d_ in range(5):
                off = (d_ - 1) * 128
                idx = off + kk - qq + 639
                bias_tiles[hi, d_] = tvec[idx]
        cabv = np.concatenate([rel[0, list(my_heads_a)],
                               rel[31, list(my_heads_a)]]).astype(np.float32)[None, :]
        rbxv = np.ascontiguousarray(rel[:, list(my_heads_a)].T)
        xs = np.ascontiguousarray(x[c // 4, g * 1024:(g + 1) * 1024, :].T)
        in_maps.append({
            "x0": xs,
            "waq": _wslice_qkv(waq_f, cols), "wak": _wslice_qkv(wak_f, cols),
            "wav": _wslice_qkv(wav_f, cols), "wao": _wslice_o(wao_f, cols),
            "wbq": _wslice_qkv(wbq_f, cols), "wbk": _wslice_qkv(wbk_f, cols),
            "wbv": _wslice_qkv(wbv_f, cols), "wbo": _wslice_o(wbo_f, cols),
            "bias_a": bias_tiles, "cab": cabv, "rbx": rbxv,
            "maskb": maskb, "ident": ident, "gwT": gw,
            "wgu_sh": gu, "w13s": w13, "wdn_sh": dn, "w2s": w2,
        })
    return in_maps


def assemble(results):
    out = np.empty((2, 4096, D), dtype=np.float32)
    for c in range(8):
        g = c % 4
        out[c // 4, g * 1024:(g + 1) * 1024, :] = results[c]["y"].T
    return out


def kernel(**inputs):
    in_maps = make_in_maps(**inputs)
    key = "nc4"
    if key not in _CACHE:
        _CACHE[key] = build_program(stages=4)
    nc = _CACHE[key]
    res = run_bass_kernel_spmd(nc, in_maps, core_ids=list(range(8)))
    return assemble(res.results)
